# version 01
# baseline (speedup 1.0000x reference)
import math
import numpy as np
import concourse.bass as bass
import concourse.mybir as mybir
from concourse.bass_utils import run_bass_kernel_spmd
from concourse.alu_op_type import AluOpType as ALU

F32 = mybir.dt.float32
F32R = mybir.dt.float32r
AF = mybir.ActivationFunctionType
AX = mybir.AxisListType

BB = 2
T = 8192
D = 1024
SEG = 1024
NSEG = 8
NT = 16
TOK = 2048
NCH = 8
LN_EPS = 1e-5
NEG = -1.0e30

VIS_SEG = [0, 7, 1, 6, 2, 5, 3, 4]
NCYC = 3

ENGINES = ("sync", "tensor", "vector", "scalar", "gpsimd")
DEDUP_OFF = False


class Sem:
    def __init__(self, name):
        self.name = name
        self.val = 0
        self.handle = None

    def inc(self, n=1):
        self.val += n
        return self.val


class Builder:
    RB = 1 << 22

    def __init__(self):
        import contextlib
        self.nc = bass.Bass("TRN2", target_bir_lowering=False)
        self.ops = {e: [] for e in ENGINES}
        self.sems = {}
        self.sb_off = 0
        self.sb_peak = 0
        self.sb_marks = []
        self.drams = {}
        self.waited = {e: {} for e in ENGINES}
        self.sbr_words = 28672
        self.sb_words = 17152
        self.sbr_off = 0
        self.sbr_peak = 0
        self._stack = contextlib.ExitStack()
        self.arena = self._stack.enter_context(
            self.nc.sbuf_tensor("arena", [128, self.sb_words], F32))
        self.arena_r = self._stack.enter_context(
            self.nc.sbuf_tensor("arenar", [128, self.sbr_words], F32R))
        self.psum = [self._stack.enter_context(
            self.nc.psum_tensor(f"ps{i}", [128, 512], F32)) for i in range(8)]
        for n in ("mm", "vv", "aa", "gg", "cc"):
            self.sem(n)

    def sem_cached(self, name):
        return self.sem(name)

    def sem(self, name):
        if name not in self.sems:
            s = Sem(name)
            s.handle = self._stack.enter_context(self.nc.semaphore(name))
            self.sems[name] = s
        return self.sems[name]

    def sb(self, words, name=""):
        words = int(words)
        self.sb_off = (self.sb_off + 15) & ~15
        o = self.sb_off
        self.sb_off += words
        self.sb_peak = max(self.sb_peak, self.sb_off)
        assert self.sb_off <= self.sb_words, f"SBUF-f overflow at {name}: {self.sb_off}"
        return o

    def sbr(self, words, name=""):
        words = int(words)
        self.sbr_off = (self.sbr_off + 15) & ~15
        o = self.sbr_off
        self.sbr_off += words
        self.sbr_peak = max(self.sbr_peak, self.sbr_off)
        assert self.sbr_off <= self.sbr_words, f"SBUF-r overflow at {name}: {self.sbr_off}"
        return o + Builder.RB

    def push(self):
        self.sb_marks.append((self.sb_off, self.sbr_off))

    def pop(self):
        self.sb_off, self.sbr_off = self.sb_marks.pop()

    def ap(self, off, n, dtype=None, part0=0, parts=128):
        if off >= Builder.RB:
            o = off - Builder.RB
            a = self.arena_r[part0:part0 + parts, o:o + n]
            if dtype is not None and dtype != F32R:
                a = a.bitcast(dtype)
        else:
            a = self.arena[part0:part0 + parts, off:off + n]
            if dtype is not None and dtype != F32:
                a = a.bitcast(dtype)
        return a

    def dram(self, name, shape, dtype=F32, kind=None):
        if kind:
            t = self.nc.dram_tensor(name, list(shape), dtype, kind=kind)
        else:
            t = self.nc.dram_tensor(name, list(shape), dtype)
        self.drams[name] = t
        return t

    def inp(self, name, shape, dtype=F32):
        t = self.nc.dram_tensor(name, list(shape), dtype, kind="ExternalInput")
        self.drams[name] = t
        return t

    def emit(self, eng, fn):
        self.ops[eng].append(fn)

    def wait(self, eng, sem, val=None):
        s = self.sems[sem] if isinstance(sem, str) else sem
        v = s.val if val is None else val
        if v <= 0:
            return
        w = self.waited[eng]
        if w.get(s.name, 0) >= v and not DEDUP_OFF:
            return
        w[s.name] = v
        self.emit(eng, lambda e, s=s, v=v: e.wait_ge(s.handle, v))

    def wait_dep(self, eng, dep):
        if dep is not None:
            sem, tgt = dep
            self.wait(eng, sem, tgt)

    def op(self, eng, fn, inc=None, incn=1):
        if inc is None:
            self.emit(eng, fn)
            return None
        s = self.sems[inc] if isinstance(inc, str) else inc
        tgt = s.inc(incn)
        def wrapped(e, fn=fn, s=s, n=incn):
            fn(e).then_inc(s.handle, n)
        self.emit(eng, wrapped)
        return tgt

    def dma(self, eng, sem, out, in_, slow=False):
        s = self.sems[sem] if isinstance(sem, str) else sem
        if slow:
            return self.op(eng, lambda e, o=out, i=in_: e.dma_start(
                out=o, in_=i, allow_slow_non_contiguous=True), inc=s, incn=16)
        return self.op(eng, lambda e, o=out, i=in_: e.dma_start(out=o, in_=i),
                       inc=s, incn=16)

    def drain(self, eng):
        self.emit(eng, lambda e: e.drain())

    def flush(self, eng, sem):
        self.wait(eng, sem)

    def barrier(self):
        for eng in ENGINES:
            for s in list(self.sems.values()):
                if s.val > 0:
                    self.wait(eng, s, s.val)

    def finalize(self):
        nc = self.nc
        with nc.Block() as block:
            @block.sync
            def _(e):
                for fn in self.ops["sync"]:
                    fn(e)
            @block.tensor
            def _(e):
                for fn in self.ops["tensor"]:
                    fn(e)
            @block.vector
            def _(e):
                for fn in self.ops["vector"]:
                    fn(e)
            @block.scalar
            def _(e):
                for fn in self.ops["scalar"]:
                    fn(e)
            @block.gpsimd
            def _(e):
                for fn in self.ops["gpsimd"]:
                    fn(e)
        self._stack.close()
        return nc



def build_program(dbg_names=(), phases=("A", "B", "M", "C", "DD")):
    b = Builder()
    nc = b.nc
    dbg_names = set(dbg_names)
    ps = b.psum

    x_loc = b.inp("x_loc", [TOK, D])
    W = {}
    for nm, shape in [
        ("Wq", [D, D]), ("Wk", [D, D]), ("Wv", [D, D]), ("Wg", [D, D]),
        ("Wp", [D, D]), ("Wck", [D, 8]), ("Wo", [D, D]),
        ("mW1", [D, 4096]), ("mW2", [4096, D]),
        ("fW1", [D, 4096]), ("fW2", [4096, D]),
        ("aWq", [D, 512]), ("aWk", [D, 512]), ("aWv", [D, D]), ("aWg", [D, D]),
        ("aWgk1", [D, 16]), ("aWgk2", [16, 512]), ("aWout", [D, D]),
        ("cnW1", [D, D]), ("cnW2", [D, 512]), ("cnW3", [512, 256]), ("cnW4", [256, 8]),
        ("gcW1", [D, 512]), ("gcW2", [512, 256]), ("gcW3", [256, 8]),
    ]:
        W[nm] = b.inp(nm, shape, F32R)
    BIAS = {}
    for nm, n in [
        ("gate_b", D), ("p_b", D), ("clock_b", 8), ("out_b", D),
        ("mb1", 4096), ("mb2", D), ("fb1", 4096), ("fb2", D),
        ("bgk2", 512), ("cnb1", D), ("cnb2", 512), ("cnb3", 256), ("cnb4", 8),
        ("gcb1", 512), ("gcb2", 256), ("gcb3", 8),
    ]:
        BIAS[nm] = b.inp(nm, [1, n], F32R)
    g1fm_i = b.inp("g1fm", [128, NCH])
    b1fm_i = b.inp("b1fm", [128, NCH])
    g2fm_i = b.inp("g2fm", [128, NCH])
    b2fm_i = b.inp("b2fm", [128, NCH])
    rmsw_i = b.inp("rmsw", [128, 256])
    ident_i = b.inp("ident", [128, 128], F32R)
    tri01_i = b.inp("tri01", [128, 128])
    dmask4_i = b.inp("dmask4", [4, 128, 512])
    killc_i = b.inp("killc", [128, 16])
    maskA_i = b.inp("maskA", [128, 64])
    maskB_i = b.inp("maskB", [128, 64])
    maskA4_i = b.inp("maskA4", [128, 32])
    maskB4_i = b.inp("maskB4", [128, 32])
    rope_ch_i = b.inp("rope_ch", [TOK, 512])
    rope_sh_i = b.inp("rope_sh", [TOK, 512])
    ar_i = b.inp("ar_in", [1, 1])
    ones_i = b.inp("ones", [128, 512], F32R)

    u_buf = b.dram("u_buf", [TOK, D])
    q1_buf = b.dram("q1_buf", [TOK, D], F32R)
    k1_buf = b.dram("k1_buf", [TOK, D], F32R)
    gcum_buf = b.dram("gcum_buf", [NCH, 128, TOK])
    pT_buf = b.dram("pT_buf", [NCH, 128, TOK])
    penc_buf = b.dram("penc_buf", [NCH, 128, TOK])
    fq_buf = b.dram("fq_buf", [NCH, 128, TOK])
    fk_buf = b.dram("fk_buf", [NCH, 128, TOK])
    qh_buf = b.dram("qh_buf", [NCH, 128, TOK], F32R)
    ag3_in = b.dram("ag3_in", [4096, D], F32R)
    ag3_out = b.dram("ag3_out", [4 * 4096, D], F32R)
    r_buf = b.dram("r_buf", [TOK, D])
    hT_buf = b.dram("hT_buf", [NCH, 128, TOK], F32R)
    a_buf = b.dram("a_buf", [4, NCH, 128, TOK], F32R)
    x1_buf = b.dram("x1_buf", [TOK, D])
    u2T_buf = b.dram("u2T_buf", [NCH, 128, TOK], F32R)
    xb_buf = b.dram("xb_buf", [TOK, D])
    q4_buf = b.dram("q4_buf", [4, 128, TOK], F32R)
    k4_buf = b.dram("k4_buf", [4, 128, TOK], F32R)
    v4_buf = b.dram("v4_buf", [TOK, D], F32R)
    g4_buf = b.dram("g4_buf", [TOK, D])
    gdec_buf = b.dram("gdec_buf", [4, 128, TOK])
    qt_buf = b.dram("qt_buf", [4, 128, TOK], F32R)
    oraw_buf = b.dram("oraw_buf", [TOK, D])
    x3_buf = b.dram("x3_buf", [TOK, D])
    dcy_buf = [b.dram("dc0_buf", [TOK, D], F32R), b.dram("dc1_buf", [TOK, D], F32R)]
    ag1_in = b.dram("ag1_in", [3, D])
    ag1_out = b.dram("ag1_out", [12, D])
    ag2_in = b.dram("ag2_in", [2, D])
    ag2_out = b.dram("ag2_out", [8, D])
    ag4_in = b.dram("ag4_in", [257, D])
    ag4_out = b.dram("ag4_out", [4 * 257, D])
    ag5_in = b.dram("ag5_in", [1, D])
    ag5_out = b.dram("ag5_out", [4, D])
    clkrow_buf = b.dram("clkrow_buf", [1, TOK])
    y_out = b.dram("y", [TOK, D], F32, kind="ExternalOutput")

    groups = [[0, 1, 2, 3], [4, 5, 6, 7]]

    dbg_tensors = {}
    def dbg(name, shape):
        if name in dbg_names:
            t = b.dram("dbg_" + name, shape, F32, kind="ExternalOutput")
            dbg_tensors[name] = t
            return t
        return None

    o_ident = b.sbr(128, "ident")
    o_ones1 = b.sbr(512, "ones")
    o_tri01 = b.sb(128, "tri01")
    o_dmask = b.sb(4 * 512, "dmask")
    o_killc = b.sb(16, "killc")
    o_maskA = b.sb(64, "maskA")
    o_maskB = b.sb(64, "maskB")
    o_maskA4 = b.sb(32, "maskA4")
    o_maskB4 = b.sb(32, "maskB4")
    o_g1 = b.sb(NCH); o_b1 = b.sb(NCH); o_g2 = b.sb(NCH); o_b2 = b.sb(NCH)
    o_rmsw = b.sb(256, "rmsw")
    o_ar = b.sb(16, "ar")
    o_clkcols = b.sb(NT, "clkcols")
    o_gjtot = b.sb(2 * NCH, "gjtot")
    o_pmaxl = b.sb(NCH, "pmaxl")
    o_petot = b.sb(2 * NCH, "petot")
    o_carryGA = b.sb(NCH); o_carryGB = b.sb(NCH)
    o_carryPA = b.sb(NCH); o_carryPB = b.sb(NCH)
    o_pmaxg = b.sb(NCH)
    o_slots = b.sb(64, "slots")
    o_Dseg = b.sb(2 * 8, "gla_Dseg")
    o_Dpre = b.sb(NT * 4, "gla_Dpre")
    o_gf = b.sb(16, "gf")
    o_cvals = b.sb(16, "cvals")

    CONSTS = [LN_EPS, 1e-24, 1e-8, 60.0, 1e-27, 0.0]
    for k, v in enumerate(CONSTS):
        b.op("vector", lambda e, k=k, v=v: e.memset(b.ap(o_cvals + k, 1), v), inc="vv")
        nc.const_aps.register(F32, v, b.ap(o_cvals + k, 1))

    bias_off = {}
    def load_biases(names, region_off, region_words, sem, gate=True):
        if gate:
            b.wait("sync", "mm", b.sems["mm"].val)
        _cur = {0: 0, 32: 0, 64: 0}
        t = None
        for nm in names:
            n = BIAS[nm].ap().shape[-1]
            row = min(_cur, key=lambda r: _cur[r])
            off = region_off + _cur[row]
            _cur[row] += (n + 15) & ~15
            assert _cur[row] <= region_words, (names, _cur)
            bias_off[nm] = (off, n, row)
            t = b.dma("sync", sem, b.ap(off, n, part0=row, parts=1), BIAS[nm].ap())
        return t

    smC = b.sem("smC")
    b.dma("sync", smC, b.ap(o_ident, 128), ident_i.ap())
    b.dma("sync", smC, b.ap(o_ones1, 512), ones_i.ap())
    b.dma("sync", smC, b.ap(o_tri01, 128), tri01_i.ap())
    for i in range(4):
        b.dma("sync", smC, b.ap(o_dmask + i * 512, 512), dmask4_i[i])
    b.dma("sync", smC, b.ap(o_killc, 16), killc_i.ap())
    b.dma("sync", smC, b.ap(o_maskA, 64), maskA_i.ap())
    b.dma("sync", smC, b.ap(o_maskB, 64), maskB_i.ap())
    b.dma("sync", smC, b.ap(o_maskA4, 32), maskA4_i.ap())
    b.dma("sync", smC, b.ap(o_maskB4, 32), maskB4_i.ap())
    for off, t in [(o_g1, g1fm_i), (o_b1, b1fm_i), (o_g2, g2fm_i), (o_b2, b2fm_i)]:
        b.dma("sync", smC, b.ap(off, NCH), t.ap())
    b.dma("sync", smC, b.ap(o_rmsw, 256), rmsw_i.ap())
    b.dma("sync", smC, b.ap(o_ar, 1, parts=1), ar_i.ap())
    lc = smC.val

    def v_op(fn, inc="vv"):
        return b.op("vector", fn, inc=inc)

    def a_op(fn, inc="aa"):
        return b.op("scalar", fn, inc=inc)

    def t_op(fn, inc="mm"):
        return b.op("tensor", fn, inc=inc)

    def stats_only(xo, wid, stat_off):
        nchunks = (wid + 511) // 512
        for ci in range(nchunks):
            w = min(512, wid - ci * 512)
            b.op("vector", lambda e, xo=xo, ci=ci, w=w, so=stat_off:
                 e.bn_stats(out=b.ap(so + 2 + 6 * ci, 6), in_=b.ap(xo + ci * 512, w)))
        b.drain("vector")
        return v_op(lambda e, so=stat_off, n=nchunks:
                    e.bn_aggr(out=b.ap(so, 2), in_=b.ap(so + 2, 6 * n)))

    def ln_stats(xo, wid, rs_off, stat_off, eps=LN_EPS):
        vt = stats_only(xo, wid, stat_off)
        b.wait("scalar", "vv", vt)
        a_op(lambda e, so=stat_off: e.activation(
            out=b.ap(so + 1, 1), in_=b.ap(so + 1, 1), func=AF.Ln, bias=eps))
        b.drain("scalar")
        at = a_op(lambda e, so=stat_off, ro=rs_off: e.activation(
            out=b.ap(ro, 1), in_=b.ap(so + 1, 1), func=AF.Exp, scale=-0.5))
        return vt, at

    class PsumMgr:
        def __init__(self):
            self.dep = [None] * 8

        def use(self, bank, eng="tensor"):
            b.wait_dep(eng, self.dep[bank])

        def done(self, bank, sem, tgt):
            self.dep[bank] = (sem, tgt)

    pm = PsumMgr()

    def transpose_affine(src_o, dst_o, g_off=None, bi_off=None, scale=None,
                         nch=NCH, banks=(6, 7), mul_off=None, mul_stride=0,
                         evac_eng="vector", add_off=None, dst_stride=128):
        ev_sem = "vv" if evac_eng == "vector" else "aa"
        mt = et = None
        for c in range(nch):
            bank = banks[(c // 4) % len(banks)]
            sl = (c % 4) * 128
            pm.use(bank)
            mt = t_op(lambda e, c=c, bank=bank, sl=sl, so=src_o:
                      e.transpose(out=ps[bank][:, sl:sl + 128].bitcast(F32R),
                                  in_=b.ap(so + c * 128, 128, F32R),
                                  identity=b.ap(o_ident, 128)))
            b.wait(evac_eng, "mm", mt)
            do = dst_o + c * dst_stride
            if g_off is not None:
                et = b.op(evac_eng, lambda e, c=c, bank=bank, sl=sl, do=do:
                          e.tensor_scalar(out=b.ap(do, 128, F32R),
                                          in0=ps[bank][:, sl:sl + 128],
                                          scalar1=b.ap(g_off + c, 1),
                                          scalar2=b.ap(bi_off + c, 1),
                                          op0=ALU.mult, op1=ALU.add), inc=ev_sem)
            elif mul_off is not None:
                et = b.op(evac_eng, lambda e, c=c, bank=bank, sl=sl, do=do:
                          e.tensor_tensor(out=b.ap(do, 128, F32R),
                                          in0=ps[bank][:, sl:sl + 128],
                                          in1=b.ap(mul_off + c * mul_stride, 128),
                                          op=ALU.mult), inc=ev_sem)
            elif add_off is not None:
                et = b.op(evac_eng, lambda e, c=c, bank=bank, sl=sl, do=do:
                          e.tensor_tensor(out=b.ap(do, 128, F32R),
                                          in0=ps[bank][:, sl:sl + 128],
                                          in1=b.ap(add_off + c * mul_stride, 128),
                                          op=ALU.add), inc=ev_sem)
            elif evac_eng == "scalar":
                et = a_op(lambda e, bank=bank, sl=sl, do=do:
                          e.activation(out=b.ap(do, 128, F32R),
                                       in_=ps[bank][:, sl:sl + 128], func=AF.Copy,
                                       scale=(1.0 if scale is None else scale)))
            else:
                et = v_op(lambda e, bank=bank, sl=sl, do=do:
                          e.tensor_scalar(out=b.ap(do, 128, F32R),
                                          in0=ps[bank][:, sl:sl + 128],
                                          scalar1=(1.0 if scale is None else scale),
                                          scalar2=None, op0=ALU.mult))
            pm.done(bank, ev_sem, et)
        return mt, et

    def load_weight_fm(wt, dout, wo, sem, nch=NCH, gate=True):
        if gate:
            b.wait("sync", "mm", b.sems["mm"].val)
        t = None
        for ci in range(nch):
            t = b.dma("sync", sem, b.ap(wo + ci * dout, dout),
                      wt[ci * 128:(ci + 1) * 128, :])
        return t

    def proj_tm(xoff, wo, dout, bias=None, banks=(0, 1), nch=NCH):
        res = []
        for h in range(0, dout, 512):
            wdt = min(512, dout - h)
            bank = banks[(h // 512) % len(banks)]
            pm.use(bank)
            mt = None
            for ci in range(nch):
                mt = t_op(lambda e, ci=ci, h=h, wdt=wdt, bank=bank, wo=wo, xoff=xoff:
                          e.matmul(ps[bank][:, 0:wdt],
                                   b.ap(xoff + ci * 128, 128),
                                   b.ap(wo + ci * dout + h, wdt),
                                   start=(ci == 0), stop=(ci == nch - 1 and bias is None)))
            if bias is not None:
                boff, bn, brow = bias_off[bias]
                mt = t_op(lambda e, h=h, wdt=wdt, bank=bank, boff=boff, brow=brow:
                          e.matmul(ps[bank][:, 0:wdt],
                                   b.ap(o_ones1, 128, part0=brow, parts=1),
                                   b.ap(boff + h, wdt, part0=brow, parts=1),
                                   start=False, stop=True))
            res.append((bank, h, wdt, mt))
        return res


    def zero_f32r(off, n):
        at = None
        for o in range(0, n, 512):
            w = min(512, n - o)
            at = a_op(lambda e, o=o, w=w: e.activation(
                out=b.ap(off + o, w), in_=b.ap(o_ones1, w, F32),
                func=AF.Copy, scale=0.0))
        b.drain("scalar")
        return at

    b.push()
    o_xnT = b.sbr(NT * 1024, "xnT")
    b.push()
    o_x = [b.sb(1024), b.sb(1024)]
    o_u = [b.sb(1024), b.sb(1024)]
    o_t = b.sbr(1024)
    o_st1 = b.sb(48)
    o_rs = b.sb(16)
    smX = [b.sem("smX0"), b.sem("smX1")]
    smUs = [b.sem("smU0"), b.sem("smU1")]
    x_dep = [None, None]
    u_dep = [None, None]
    t_dep = [None]

    for i in range(NT):
        sl = i % 2
        b.wait_dep("sync", x_dep[sl])
        lt = b.dma("sync", smX[sl], b.ap(o_x[sl], 1024), x_loc[i * 128:(i + 1) * 128, :])
        b.wait("vector", smX[sl], lt)
        _, at1 = ln_stats(o_x[sl], 1024, o_rs, o_st1)
        b.wait_dep("vector", u_dep[sl])
        b.wait("vector", "aa", at1)
        ut = v_op(lambda e, sl=sl: e.tensor_scalar(
            out=b.ap(o_u[sl], 1024), in0=b.ap(o_x[sl], 1024),
            scalar1=b.ap(o_st1, 1), scalar2=b.ap(o_rs, 1),
            op0=ALU.subtract, op1=ALU.mult))
        x_dep[sl] = ("vv", ut)
        b.wait("gpsimd", "vv", ut)
        st_u = b.dma("gpsimd", smUs[sl], u_buf[i * 128:(i + 1) * 128, :], b.ap(o_u[sl], 1024))
        u_dep[sl] = (smUs[sl], st_u)
        b.drain("vector")
        _, at2 = ln_stats(o_u[sl], 1024, o_rs + 1, o_st1 + 16)
        b.wait_dep("vector", t_dep[0])
        b.wait("vector", "aa", at2)
        tt = v_op(lambda e, sl=sl: e.tensor_scalar(
            out=b.ap(o_t, 1024, F32R), in0=b.ap(o_u[sl], 1024),
            scalar1=b.ap(o_st1 + 16, 1), scalar2=b.ap(o_rs + 1, 1),
            op0=ALU.subtract, op1=ALU.mult))
        b.wait("tensor", smC, lc)
        b.wait("tensor", "vv", tt)
        mt_last, _ = transpose_affine(o_t, o_xnT + i * 1024, g_off=o_g1, bi_off=o_b1)
        t_dep[0] = ("mm", mt_last)
    b.pop()
    b.barrier()
    xnT_ready = b.sems["vv"].val

    b.push()
    o_wck = b.sbr(NCH * 8, "wck")
    o_biasA = b.sbr(64, "biasA")
    o_cktmp = b.sb(16)
    smW = b.sem("smW")
    load_biases(["clock_b"], o_biasA, 64, smW)
    ld_wck = load_weight_fm(W["Wck"], 8, o_wck, smW)
    b.wait("tensor", smW, ld_wck)
    b.wait("tensor", "vv", xnT_ready)
    for i in range(NT):
        pm.use(2)
        mt = None
        for ci in range(NCH):
            mt = t_op(lambda e, ci=ci, i=i: e.matmul(
                ps[2][:, 0:8], b.ap(o_xnT + i * 1024 + ci * 128, 128),
                b.ap(o_wck + ci * 8, 8), start=(ci == 0), stop=False))
        boff, _, brow = bias_off["clock_b"]
        mt = t_op(lambda e, boff=boff, brow=brow: e.matmul(
            ps[2][:, 0:8], b.ap(o_ones1, 128, part0=brow, parts=1),
            b.ap(boff, 8, part0=brow, parts=1), start=False, stop=True))
        b.wait("scalar", "mm", mt)
        a_op(lambda e: e.activation(out=b.ap(o_cktmp, 1), in_=ps[2][:, 0:1],
                                    func=AF.Exp))
        b.drain("scalar")
        at = a_op(lambda e, i=i: e.activation(out=b.ap(o_clkcols + i, 1),
                                              in_=b.ap(o_cktmp, 1),
                                              func=AF.Ln, bias=1.0))
        b.drain("scalar")
        pm.done(2, "aa", at)
    clk_ready = b.sems["aa"].val
    b.pop()
    b.barrier()

    smT = b.sem("smT")
    smRs = [b.sem("smR0"), b.sem("smR1")]

    def rope_block(i, o_ropein, o_rc, o_rs_t, o_sc1, o_sc2, o_rst, o_rrs,
                   dst_o, tab_dep, extra_col=None):
        b.drain("vector")
        stats_only(o_ropein, 1024, o_rst)
        b.drain("vector")
        vt0 = v_op(lambda e: e.tensor_scalar(out=b.ap(o_rst + 2, 1), in0=b.ap(o_rst, 1),
                                             scalar1=b.ap(o_rst, 1), scalar2=b.ap(o_rst + 1, 1),
                                             op0=ALU.mult, op1=ALU.add))
        b.wait("scalar", "vv", vt0)
        a_op(lambda e: e.activation(out=b.ap(o_rst + 3, 1), in_=b.ap(o_rst + 2, 1),
                                    func=AF.Ln, bias=1e-24, scale=1024.0))
        b.drain("scalar")
        at2 = a_op(lambda e: e.activation(out=b.ap(o_rrs, 1), in_=b.ap(o_rst + 3, 1),
                                          func=AF.Exp, scale=-0.5))
        b.wait("vector", "aa", at2)
        scale_col = o_rrs
        if extra_col is not None:
            v_op(lambda e, xc=extra_col: e.tensor_scalar(
                out=b.ap(o_rrs + 1, 1), in0=b.ap(o_rrs, 1), scalar1=b.ap(xc, 1),
                scalar2=None, op0=ALU.mult))
            scale_col = o_rrs + 1
            b.drain("vector")
        b.wait_dep("sync", tab_dep[0])
        b.dma("sync", smT, b.ap(o_rc, 512), rope_ch_i[i * 128:(i + 1) * 128, :])
        lt2 = b.dma("sync", smT, b.ap(o_rs_t, 512), rope_sh_i[i * 128:(i + 1) * 128, :])
        b.wait("vector", smT, lt2)
        def EAP():
            return b.ap(o_ropein, 1024).rearrange("p (a two) -> p a two", two=2)[:, :, 0]
        def OAP():
            return b.ap(o_ropein, 1024).rearrange("p (a two) -> p a two", two=2)[:, :, 1]
        def DD(par):
            return b.ap(dst_o, 1024).rearrange("p (a two) -> p a two", two=2)[:, :, par]
        v_op(lambda e: e.scalar_tensor_tensor(out=b.ap(o_sc1, 512), in0=OAP(),
                                              scalar=b.ap(scale_col, 1), in1=b.ap(o_rs_t, 512),
                                              op0=ALU.mult, op1=ALU.mult))
        v_op(lambda e: e.scalar_tensor_tensor(out=b.ap(o_sc2, 512), in0=EAP(),
                                              scalar=b.ap(scale_col, 1), in1=b.ap(o_rc, 512),
                                              op0=ALU.mult, op1=ALU.mult))
        b.drain("vector")
        v_op(lambda e: e.tensor_tensor(out=DD(0), in0=b.ap(o_sc2, 512),
                                       in1=b.ap(o_sc1, 512), op=ALU.subtract))
        b.drain("vector")
        v_op(lambda e: e.scalar_tensor_tensor(out=b.ap(o_sc1, 512), in0=OAP(),
                                              scalar=b.ap(scale_col, 1), in1=b.ap(o_rc, 512),
                                              op0=ALU.mult, op1=ALU.mult))
        v_op(lambda e: e.scalar_tensor_tensor(out=b.ap(o_sc2, 512), in0=EAP(),
                                              scalar=b.ap(scale_col, 1), in1=b.ap(o_rs_t, 512),
                                              op0=ALU.mult, op1=ALU.mult))
        b.drain("vector")
        vt = v_op(lambda e: e.tensor_tensor(out=DD(1), in0=b.ap(o_sc2, 512),
                                            in1=b.ap(o_sc1, 512), op=ALU.add))
        tab_dep[0] = ("vv", vt)
        return vt

    for name, wdram, outbuf, use_clock in (("q", W["Wq"], q1_buf, False),
                                           ("k", W["Wk"], k1_buf, True)):
        b.push()
        o_w = b.sbr(NCH * 1024, "w_" + name)
        o_ropein = b.sb(1024)
        o_rc = b.sb(512); o_rs_t = b.sb(512)
        o_sc1 = b.sb(512); o_sc2 = b.sb(512)
        o_rout = [b.sb(1024), b.sb(1024)]
        o_rst = b.sb(16); o_rrs = b.sb(16)
        ldw = load_weight_fm(wdram, 1024, o_w, smW)
        b.wait("tensor", smW, ldw)
        b.wait("tensor", "vv", xnT_ready)
        rout_dep = [None, None]
        tab_dep = [None]
        b.wait("vector", "aa", clk_ready)
        for i in range(NT):
            sl = i % 2
            prs = proj_tm(o_xnT + i * 1024, o_w, 1024, banks=(0, 1))
            for (bank, h, wdt, mt) in prs:
                b.wait("vector", "mm", mt)
                et = v_op(lambda e, bank=bank, h=h, wdt=wdt: e.tensor_copy(
                    out=b.ap(o_ropein + h, wdt), in_=ps[bank][:, 0:wdt]))
                pm.done(bank, "vv", et)
            b.wait_dep("vector", rout_dep[sl])
            vt = rope_block(i, o_ropein, o_rc, o_rs_t, o_sc1, o_sc2, o_rst, o_rrs,
                            o_rout[sl], tab_dep,
                            extra_col=(o_clkcols + i) if use_clock else None)
            b.wait("gpsimd", "vv", vt)
            stt = b.dma("gpsimd", smRs[sl], outbuf[i * 128:(i + 1) * 128, :],
                        b.ap(o_rout[sl], 1024))
            rout_dep[sl] = (smRs[sl], stt)
        b.pop()
        b.barrier()

    b.push()
    o_w = b.sbr(NCH * 1024, "w_v")
    o_vout = [b.sbr(1024), b.sbr(1024)]
    smVs = [b.sem("smV0"), b.sem("smV1")]
    ldw = load_weight_fm(W["Wv"], 1024, o_w, smW)
    b.wait("tensor", smW, ldw)
    b.wait("tensor", "vv", xnT_ready)
    vout_dep = [None, None]
    for i in range(NT):
        sl = i % 2
        prs = proj_tm(o_xnT + i * 1024, o_w, 1024, banks=(0, 1))
        b.wait_dep("vector", vout_dep[sl])
        et = None
        for (bank, h, wdt, mt) in prs:
            b.wait("vector", "mm", mt)
            et = v_op(lambda e, bank=bank, h=h, wdt=wdt, sl=sl: e.tensor_copy(
                out=b.ap(o_vout[sl] + h, wdt), in_=ps[bank][:, 0:wdt]))
            pm.done(bank, "vv", et)
        b.wait("gpsimd", "vv", et)
        stt = b.dma("gpsimd", smVs[sl], ag3_in[2048 + i * 128: 2048 + (i + 1) * 128, :],
                    b.ap(o_vout[sl], 1024))
        vout_dep[sl] = (smVs[sl], stt)
    b.pop()
    b.barrier()

    def xnT_pair_rhs(p, ci):
        return b.ap(o_xnT + 2 * p * 1024, 2048).rearrange(
            "q (t c j) -> q t c j", t=2, c=NCH)[:, :, ci, :]

    def proj_fm_pair(p, wo, co, bank, bias=None, nch=NCH):
        pm.use(bank)
        mt = None
        for ci in range(nch):
            mt = t_op(lambda e, ci=ci, co=co, p=p, bank=bank, wo=wo:
                      e.matmul(ps[bank][:, 0:256],
                               b.ap(wo + ci * 1024 + co * 128, 128),
                               xnT_pair_rhs(p, ci),
                               start=(ci == 0), stop=(ci == nch - 1 and bias is None)))
        if bias is not None:
            boff, bn, brow = bias_off[bias]
            mt = t_op(lambda e, co=co, bank=bank, boff=boff, brow=brow:
                      e.matmul(ps[bank][:, 0:256],
                               b.ap(boff + co * 128, 128, part0=brow, parts=1),
                               b.ap(o_ones1, 256, part0=brow, parts=1),
                               start=False, stop=True))
        return mt

    b.push()
    o_w = b.sbr(NCH * 1024, "w_gate")
    o_biasG = b.sbr(1024, "biasG")
    o_spe = b.sb(256)
    o_gT = [b.sb(NCH * 256), b.sb(NCH * 256)]
    smGs = [b.sem("smG0"), b.sem("smG1")]
    load_biases(["gate_b"], o_biasG, 1024, smW)
    ldw = load_weight_fm(W["Wg"], 1024, o_w, smW)
    b.wait("tensor", smW, ldw)
    b.wait("tensor", "vv", xnT_ready)
    gT_dep = [None, None]
    gT_scan_dep = [None, None]
    for p in range(8):
        sl = p % 2
        seg = p // 4
        stt = None
        for co in range(NCH):
            bank = 3 + (co % 2)
            mt = proj_fm_pair(p, o_w, co, bank, bias="gate_b")
            b.wait("scalar", "mm", mt)
            if co == 0:
                b.wait_dep("scalar", gT_dep[sl])
                b.wait_dep("scalar", gT_scan_dep[1 - sl])
            a_op(lambda e, bank=bank: e.activation(out=b.ap(o_spe, 256),
                                                   in_=ps[bank][:, 0:256], func=AF.Exp))
            b.drain("scalar")
            at = a_op(lambda e, co=co, sl=sl: e.activation(
                out=b.ap(o_gT[sl] + co * 256, 256), in_=b.ap(o_spe, 256),
                func=AF.Ln, bias=1.0))
            b.drain("scalar")
            pm.done(bank, "aa", at)
            b.wait("vector", "aa", at)
            if p % 4 == 0:
                init = 0.0
            else:
                init = b.ap(o_gT[1 - sl] + co * 256 + 255, 1)
            vt = v_op(lambda e, co=co, sl=sl, init=init: e.tensor_tensor_scan(
                out=b.ap(o_gT[sl] + co * 256, 256),
                data0=b.ap(o_gT[sl] + co * 256, 256),
                data1=b.ap(o_gT[sl] + co * 256, 256),
                initial=init, op0=ALU.add, op1=ALU.bypass))
            gT_scan_dep[sl] = ("vv", vt)
            b.drain("vector")
            b.wait("gpsimd", "vv", vt)
            stt = b.dma("gpsimd", smGs[sl], gcum_buf[co, :, 256 * p:256 * (p + 1)],
                        b.ap(o_gT[sl] + co * 256, 256))
        gT_dep[sl] = (smGs[sl], stt)
        if p % 4 == 3:
            for co in range(NCH):
                v_op(lambda e, co=co, sl=sl, seg=seg: e.tensor_copy(
                    out=b.ap(o_gjtot + seg * NCH + co, 1),
                    in_=b.ap(o_gT[sl] + co * 256 + 255, 1)))
            b.drain("vector")
    gjtot_ready = b.sems["vv"].val
    b.pop()
    b.barrier()

    b.push()
    o_w = b.sbr(NCH * 1024, "w_p")
    o_biasP = b.sbr(1024, "biasP")
    o_pTs = [b.sb(NCH * 256), b.sb(NCH * 256)]
    o_pmx_parts = b.sb(64, "pmx_parts")
    smPs = [b.sem("smP0"), b.sem("smP1")]
    load_biases(["p_b"], o_biasP, 1024, smW)
    ldw = load_weight_fm(W["Wp"], 1024, o_w, smW)
    b.wait("tensor", smW, ldw)
    b.wait("tensor", "vv", xnT_ready)
    pT_dep = [None, None]
    for p in range(8):
        sl = p % 2
        stt = None
        for co in range(NCH):
            bank = 3 + (co % 2)
            mt = proj_fm_pair(p, o_w, co, bank, bias="p_b")
            b.wait("vector", "mm", mt)
            if co == 0:
                b.wait_dep("vector", pT_dep[sl])
            et = v_op(lambda e, co=co, sl=sl, bank=bank: e.tensor_copy(
                out=b.ap(o_pTs[sl] + co * 256, 256), in_=ps[bank][:, 0:256]))
            rt = v_op(lambda e, co=co, p=p, bank=bank: e.tensor_reduce(
                out=b.ap(o_pmx_parts + co * 8 + p, 1),
                in_=ps[bank][:, 0:256], axis=AX.X, op=ALU.max))
            pm.done(bank, "vv", rt)
        b.drain("vector")
        vt = b.sems["vv"].val
        b.wait("gpsimd", "vv", vt)
        for co in range(NCH):
            stt = b.dma("gpsimd", smPs[sl], pT_buf[co, :, 256 * p:256 * (p + 1)],
                        b.ap(o_pTs[sl] + co * 256, 256))
        pT_dep[sl] = (smPs[sl], stt)
    b.drain("vector")
    vt_pmax = v_op(lambda e: e.tensor_reduce(
        out=b.ap(o_pmaxl, NCH),
        in_=b.ap(o_pmx_parts, 64).rearrange("p (c j) -> p c j", j=8),
        axis=AX.X, op=ALU.max))
    b.pop()
    b.barrier()

    smK = b.sem("smK")
    b.wait("gpsimd", "aa", clk_ready)
    for i in range(NT):
        b.dma("gpsimd", smK, clkrow_buf[0:1, i * 128:(i + 1) * 128],
              b.ap(o_clkcols + i, 1))
    clkrow_st = smK.val

    def fm_row_ap(dram_t, row):
        return dram_t.ap()[row:row + 1, :].rearrange(
            "one (c p) -> (one p) c", p=128)

    b.wait("gpsimd", "vv", gjtot_ready)
    b.wait("gpsimd", "vv", vt_pmax)
    b.dma("gpsimd", smK, fm_row_ap(ag1_in, 0), b.ap(o_gjtot, NCH), slow=True)
    b.dma("gpsimd", smK, fm_row_ap(ag1_in, 1), b.ap(o_gjtot + NCH, NCH), slow=True)
    ag1_st = b.dma("gpsimd", smK, fm_row_ap(ag1_in, 2), b.ap(o_pmaxl, NCH), slow=True)
    b.wait("gpsimd", smK, ag1_st)
    cc1 = b.op("gpsimd", lambda e: e.collective_compute(
        "AllGather", ALU.bypass, replica_groups=groups,
        ins=[ag1_in.ap()], outs=[ag1_out.ap()]), inc="cc")

    def seg_row1(j):
        rank, half = (j, 0) if j <= 3 else (7 - j, 1)
        return rank * 3 + half

    b.push()
    o_Gt = b.sb(1024)
    o_pe = b.sb(1024)
    o_t2 = b.sb(1024)
    o_clkb = b.sb(1024)
    o_pcum = b.sb(1024)
    smS = b.sem("smS")
    b.wait("sync", "cc", cc1)
    lt = None
    for j in range(8):
        lt = b.dma("sync", smS, b.ap(o_slots, 64).rearrange("p (c j) -> p c j", j=8)[:, :, j],
                   fm_row_ap(ag1_out, seg_row1(j)), slow=True)
    b.flush("sync", smS)
    b.wait("vector", smS, lt)
    b.wait("vector", smC, lc)
    v_op(lambda e: e.tensor_tensor(out=b.ap(o_t2, 64), in0=b.ap(o_slots, 64),
                                   in1=b.ap(o_maskA, 64), op=ALU.mult))
    b.drain("vector")
    v_op(lambda e: e.tensor_reduce(
        out=b.ap(o_carryGA, NCH),
        in_=b.ap(o_t2, 64).rearrange("p (c j) -> p c j", j=8), axis=AX.X, op=ALU.add))
    b.drain("vector")
    v_op(lambda e: e.tensor_tensor(out=b.ap(o_t2, 64), in0=b.ap(o_slots, 64),
                                   in1=b.ap(o_maskB, 64), op=ALU.mult))
    b.drain("vector")
    v_op(lambda e: e.tensor_reduce(
        out=b.ap(o_carryGB, NCH),
        in_=b.ap(o_t2, 64).rearrange("p (c j) -> p c j", j=8), axis=AX.X, op=ALU.add))
    vt_cG = b.sems["vv"].val
    b.drain("vector")
    b.wait("sync", "vv", vt_cG)
    for j in range(4):
        lt = b.dma("sync", smS, b.ap(o_t2, 32).rearrange("p (c j) -> p c j", j=4)[:, :, j],
                   fm_row_ap(ag1_out, j * 3 + 2), slow=True)
    b.flush("sync", smS)
    b.wait("vector", smS, lt)
    v_op(lambda e: e.tensor_reduce(
        out=b.ap(o_pmaxg, NCH),
        in_=b.ap(o_t2, 32).rearrange("p (c j) -> p c j", j=4), axis=AX.X, op=ALU.max))
    b.drain("vector")
    v_op(lambda e: e.tensor_scalar(out=b.ap(o_pmaxg, NCH), in0=b.ap(o_pmaxg, NCH),
                                   scalar1=-1.0, scalar2=None, op0=ALU.mult))
    b.drain("vector")

    ld_dep = [None]
    st_dep = [None]
    smA2 = b.sem("smA2")
    for s in range(2):
        b.wait_dep("sync", ld_dep[0])
        b.wait("sync", smK, ag1_st)
        lt_clk = b.dma("sync", smS, b.ap(o_clkb, 1024),
                       clkrow_buf.ap()[0:1, s * 1024:(s + 1) * 1024].broadcast_to([128, 1024]))
        b.flush("sync", smS)
        for c in range(NCH):
            b.wait_dep("sync", ld_dep[0])
            b.wait_dep("sync", st_dep[0])
            lt1 = b.dma("sync", smS, b.ap(o_Gt, 1024), gcum_buf[c, :, s * 1024:(s + 1) * 1024])
            lt2 = b.dma("sync", smS, b.ap(o_pe, 1024), pT_buf[c, :, s * 1024:(s + 1) * 1024])
            b.wait("vector", smS, lt2)
            carry = (o_carryGA if s == 0 else o_carryGB) + c
            vt = v_op(lambda e, carry=carry: e.tensor_scalar(
                out=b.ap(o_Gt, 1024), in0=b.ap(o_Gt, 1024),
                scalar1=b.ap(carry, 1), scalar2=60.0, op0=ALU.add, op1=ALU.min))
            b.wait("scalar", "vv", vt)
            a_op(lambda e: e.activation(out=b.ap(o_Gt, 1024), in_=b.ap(o_Gt, 1024),
                                        func=AF.Exp, scale=-1.0))
            b.drain("scalar")
            aG = b.sems["aa"].val
            b.wait("gpsimd", "aa", aG)
            st_G = b.dma("gpsimd", smA2, gcum_buf[c, :, s * 1024:(s + 1) * 1024],
                         b.ap(o_Gt, 1024))
            a_op(lambda e: e.activation(out=b.ap(o_t2, 1024), in_=b.ap(o_Gt, 1024),
                                        func=AF.Ln, bias=1e-8))
            b.drain("scalar")
            a_op(lambda e: e.activation(out=b.ap(o_t2, 1024), in_=b.ap(o_t2, 1024),
                                        func=AF.Exp, scale=-1.0))
            b.drain("scalar")
            at_pe = a_op(lambda e, c=c: e.activation(out=b.ap(o_pe, 1024), in_=b.ap(o_pe, 1024),
                                                     func=AF.Exp, bias=b.ap(o_pmaxg + c, 1)))
            b.wait("vector", "aa", at_pe)
            vt_fk = v_op(lambda e: e.tensor_tensor(out=b.ap(o_t2, 1024), in0=b.ap(o_pe, 1024),
                                                   in1=b.ap(o_t2, 1024), op=ALU.mult))
            b.wait("gpsimd", "vv", vt_fk)
            st_fk = b.dma("gpsimd", smA2, fk_buf[c, :, s * 1024:(s + 1) * 1024],
                          b.ap(o_t2, 1024))
            b.wait("vector", smS, lt_clk)
            b.drain("vector")
            v_op(lambda e: e.tensor_tensor(out=b.ap(o_pe, 1024), in0=b.ap(o_pe, 1024),
                                           in1=b.ap(o_clkb, 1024), op=ALU.mult))
            b.drain("vector")
            v_op(lambda e: e.tensor_tensor_scan(
                out=b.ap(o_pcum, 1024), data0=b.ap(o_pe, 1024), data1=b.ap(o_pe, 1024),
                initial=0.0, op0=ALU.add, op1=ALU.bypass))
            b.drain("vector")
            vt_tot = v_op(lambda e, s=s, c=c: e.tensor_copy(
                out=b.ap(o_petot + s * NCH + c, 1), in_=b.ap(o_pcum + 1023, 1)))
            ld_dep[0] = ("vv", vt_tot)
            b.wait("gpsimd", "vv", vt_tot)
            st_pc = b.dma("gpsimd", smA2, penc_buf[c, :, s * 1024:(s + 1) * 1024],
                          b.ap(o_pcum, 1024))
            st_dep[0] = (smA2, st_pc)
    petot_ready = b.sems["vv"].val

    b.wait("gpsimd", "vv", petot_ready)
    b.dma("gpsimd", smK, fm_row_ap(ag2_in, 0), b.ap(o_petot, NCH), slow=True)
    ag2_st = b.dma("gpsimd", smK, fm_row_ap(ag2_in, 1), b.ap(o_petot + NCH, NCH), slow=True)
    b.wait("gpsimd", smK, ag2_st)
    cc2 = b.op("gpsimd", lambda e: e.collective_compute(
        "AllGather", ALU.bypass, replica_groups=groups,
        ins=[ag2_in.ap()], outs=[ag2_out.ap()]), inc="cc")

    b.wait("sync", "cc", cc2)
    b.wait_dep("sync", ld_dep[0])
    lt = None
    for j in range(8):
        rank, half = (j, 0) if j <= 3 else (7 - j, 1)
        lt = b.dma("sync", smS, b.ap(o_slots, 64).rearrange("p (c j) -> p c j", j=8)[:, :, j],
                   fm_row_ap(ag2_out, rank * 2 + half), slow=True)
    b.flush("sync", smS)
    b.wait("vector", smS, lt)
    b.wait_dep("vector", st_dep[0])
    v_op(lambda e: e.tensor_tensor(out=b.ap(o_t2, 64), in0=b.ap(o_slots, 64),
                                   in1=b.ap(o_maskA, 64), op=ALU.mult))
    b.drain("vector")
    v_op(lambda e: e.tensor_reduce(
        out=b.ap(o_carryPA, NCH),
        in_=b.ap(o_t2, 64).rearrange("p (c j) -> p c j", j=8), axis=AX.X, op=ALU.add))
    b.drain("vector")
    v_op(lambda e: e.tensor_tensor(out=b.ap(o_t2, 64), in0=b.ap(o_slots, 64),
                                   in1=b.ap(o_maskB, 64), op=ALU.mult))
    b.drain("vector")
    v_op(lambda e: e.tensor_reduce(
        out=b.ap(o_carryPB, NCH),
        in_=b.ap(o_t2, 64).rearrange("p (c j) -> p c j", j=8), axis=AX.X, op=ALU.add))
    carryP_ready = b.sems["vv"].val
    b.drain("vector")

    for s in range(2):
        for c in range(NCH):
            b.wait_dep("sync", ld_dep[0])
            b.wait_dep("sync", st_dep[0])
            lt1 = b.dma("sync", smS, b.ap(o_Gt, 1024), gcum_buf[c, :, s * 1024:(s + 1) * 1024])
            lt2 = b.dma("sync", smS, b.ap(o_pcum, 1024), penc_buf[c, :, s * 1024:(s + 1) * 1024])
            carry = (o_carryPA if s == 0 else o_carryPB) + c
            b.wait("vector", smS, lt2)
            b.wait("vector", "vv", carryP_ready)
            vt = v_op(lambda e, carry=carry: e.tensor_scalar(
                out=b.ap(o_pcum, 1024), in0=b.ap(o_pcum, 1024),
                scalar1=b.ap(carry, 1), scalar2=None, op0=ALU.add))
            b.wait("scalar", "vv", vt)
            a_op(lambda e: e.activation(out=b.ap(o_pcum, 1024), in_=b.ap(o_pcum, 1024),
                                        func=AF.Ln, bias=1e-8))
            b.drain("scalar")
            at = a_op(lambda e: e.activation(out=b.ap(o_pcum, 1024), in_=b.ap(o_pcum, 1024),
                                             func=AF.Exp, scale=-1.0))
            b.wait("vector", "aa", at)
            b.wait("vector", smS, lt2)
            vt_fq = v_op(lambda e: e.tensor_tensor(out=b.ap(o_Gt, 1024), in0=b.ap(o_Gt, 1024),
                                                   in1=b.ap(o_pcum, 1024), op=ALU.mult))
            ld_dep[0] = ("vv", vt_fq)
            b.wait("gpsimd", "vv", vt_fq)
            st_fq = b.dma("gpsimd", smA2, fq_buf[c, :, s * 1024:(s + 1) * 1024],
                          b.ap(o_Gt, 1024))
            st_dep[0] = (smA2, st_fq)
    b.pop()
    b.barrier()
    a2_dep = (smA2, b.sems["smA2"].val)

    b.push()
    o_qk = [b.sbr(1024), b.sbr(1024)]
    o_f = b.sb(1024)
    o_a3o = [b.sbr(1024), b.sbr(1024)]
    smQ3 = b.sem("smQ3"); smK3 = b.sem("smK3"); smF3 = b.sem("smF3")
    smO3 = [b.sem("smO30"), b.sem("smO31")]
    qk_dep = [None, None]
    f_dep = [None]
    out_dep = [None, None]
    b.wait_dep("sync", a2_dep)
    for i in range(NT):
        s, ii = i // 8, i % 8
        for which in (0, 1):
            sl = which
            sm_ld = smQ3 if which == 0 else smK3
            b.wait_dep("sync", qk_dep[sl])
            src = q1_buf if which == 0 else k1_buf
            fbuf = fq_buf if which == 0 else fk_buf
            lt = b.dma("sync", sm_ld, b.ap(o_qk[sl], 1024),
                       src[i * 128:(i + 1) * 128, :])
            b.wait_dep("sync", f_dep[0])
            ltf = b.dma("sync", smF3, b.ap(o_f, 1024),
                        fbuf.ap().rearrange("c p t -> p c t")[:, :, i * 128:(i + 1) * 128])
            b.wait("tensor", sm_ld, lt)
            b.wait("vector", smF3, ltf)
            b.wait_dep("vector", out_dep[sl])
            mt_last, et_last = transpose_affine(
                o_qk[sl], o_a3o[sl], mul_off=o_f, mul_stride=128)
            qk_dep[sl] = ("mm", mt_last)
            f_dep[0] = ("vv", et_last)
            b.wait("gpsimd", "vv", et_last)
            stt = None
            for c in range(NCH):
                if which == 0:
                    stt = b.dma("gpsimd", smO3[sl], qh_buf[c, :, i * 128:(i + 1) * 128],
                                b.ap(o_a3o[sl] + c * 128, 128))
                else:
                    stt = b.dma("gpsimd", smO3[sl],
                                ag3_in[s * 1024 + c * 128: s * 1024 + (c + 1) * 128,
                                       ii * 128:(ii + 1) * 128],
                                b.ap(o_a3o[sl] + c * 128, 128))
            out_dep[sl] = (smO3[sl], stt)
    b.pop()
    b.pop()
    b.barrier()

    b.wait_dep("gpsimd", out_dep[0])
    b.wait_dep("gpsimd", out_dep[1])
    b.wait_dep("gpsimd", vout_dep[0])
    b.wait_dep("gpsimd", vout_dep[1])
    cc3 = b.op("gpsimd", lambda e: e.collective_compute(
        "AllGather", ALU.bypass, replica_groups=groups,
        ins=[ag3_in.ap()], outs=[ag3_out.ap()]), inc="cc")

    post_r_dep = [None, None]
    post_h_dep = [None, None]
    if "B" in phases:
      b.push()
      o_O = b.sb(8 * 1024, "O_seg")
      o_sm = b.sb(1024, "smask")
      o_m = b.sb(16)
      o_sumc = b.sb(16)
      o_sc = b.sb(64)
      o_x2 = b.sb(1024)
      o_r3 = b.sb(1024)
      o_uld = b.sb(1024)
      o_xld = b.sb(1024)
      o_rsB = b.sb(16)
      o_stB = b.sb(48)
      smQH = b.sem("smQH")
      smKV = [b.sem("smKV0"), b.sem("smKV1")]
      smLu = b.sem("smLu"); smLx = b.sem("smLx")
      smPo = b.sem("smPo")

      def flash_seg(s):
        b.push()
        o_qhB = b.sbr(8 * 1024, "qh_seg")
        o_K = [b.sbr(NCH * 512), b.sbr(NCH * 512)]
        o_V = [b.sbr(4 * 1024), b.sbr(4 * 1024)]
        o_probs = b.sbr(1024, "probs")
        o_pt = b.sbr(1024, "pT")
        qh_lt = None
        b.wait("sync", "mm", b.sems["mm"].val)
        for i in range(8):
            gi = s * 8 + i
            qh_lt = b.dma("sync", smQH, b.ap(o_qhB + i * 1024, 1024),
                          qh_buf.ap().rearrange("c p t -> p c t")[:, :, gi * 128:(gi + 1) * 128])
        mz1 = b.op("gpsimd", lambda e: e.memset(b.ap(o_O, 8 * 1024), 0.0), inc="gg")
        mz2 = b.op("gpsimd", lambda e: e.memset(b.ap(o_m, 8), -3.0e38), inc="gg")
        mz3 = b.op("gpsimd", lambda e: e.memset(b.ap(o_sumc, 8), 0.0), inc="gg")
        gg0 = b.sems["gg"].val
        b.wait("tensor", smQH, qh_lt)
        b.wait("tensor", "gg", gg0)
        b.wait("vector", "gg", gg0)
        kv_dep = [None, None]
        flash_aa = [0]

        def load_kv_block(slot, krow0, vrow0, kcol0, srct):
            b.wait_dep("sync", kv_dep[slot])
            lt = None
            for c in range(NCH):
                lt = b.dma("sync", smKV[slot], b.ap(o_K[slot] + c * 512, 512),
                           srct[krow0 + c * 128: krow0 + (c + 1) * 128,
                                kcol0:kcol0 + 512])
            for cc in range(4):
                lt = b.dma("sync", smKV[slot], b.ap(o_V[slot] + cc * 1024, 1024),
                           srct[vrow0 + cc * 128: vrow0 + (cc + 1) * 128, :])
            return lt

        def visit(blocks, tiles, kill_col, local_first):
            for i, nb in tiles:
                b.wait("vector", "aa", flash_aa[0])
                sco = []
                for bi in range(nb):
                    slot, lt, maskfn = blocks[bi]
                    bank = 4 + bi
                    pm.use(bank)
                    b.wait("tensor", smKV[slot], lt)
                    mt = None
                    for ci in range(NCH):
                        mt = t_op(lambda e, i=i, ci=ci, slot=slot, bank=bank:
                                  e.matmul(ps[bank][:, 0:512],
                                           b.ap(o_qhB + i * 1024 + ci * 128, 128),
                                           b.ap(o_K[slot] + ci * 512, 512),
                                           start=(ci == 0), stop=(ci == NCH - 1)))
                    sco.append((bank, mt, maskfn(i) if maskfn else None))
                lmax_cols = []
                for bi, (bank, mt, moff) in enumerate(sco):
                    b.wait("vector", "mm", mt)
                    if moff is not None:
                        v_op(lambda e, bank=bank, moff=moff: e.tensor_tensor(
                            out=b.ap(o_sm + 0, 512), in0=ps[bank][:, 0:512],
                            in1=b.ap(moff, 512), op=ALU.add))
                        b.drain("vector")
                        rt = v_op(lambda e, bi=bi: e.tensor_reduce(
                            out=b.ap(o_sc + bi, 1), in_=b.ap(o_sm, 512),
                            axis=AX.X, op=ALU.max))
                    else:
                        rt = v_op(lambda e, bank=bank, bi=bi: e.tensor_reduce(
                            out=b.ap(o_sc + bi, 1), in_=ps[bank][:, 0:512],
                            axis=AX.X, op=ALU.max))
                    lmax_cols.append(rt)
                b.drain("vector")
                if nb == 2:
                    v_op(lambda e: e.tensor_scalar(
                        out=b.ap(o_sc + 0, 1), in0=b.ap(o_sc + 0, 1),
                        scalar1=b.ap(o_sc + 1, 1), scalar2=None, op0=ALU.max))
                    b.drain("vector")
                v_op(lambda e, kill_col=kill_col: e.tensor_scalar(
                    out=b.ap(o_sc + 2, 1), in0=b.ap(o_sc + 0, 1),
                    scalar1=b.ap(kill_col, 1), scalar2=b.ap(o_m + i, 1),
                    op0=ALU.add, op1=ALU.max))
                b.drain("vector")
                v_op(lambda e, i=i: e.tensor_scalar(
                    out=b.ap(o_sc + 3, 1), in0=b.ap(o_m + i, 1),
                    scalar1=b.ap(o_sc + 2, 1), scalar2=None, op0=ALU.subtract))
                v_op(lambda e, kill_col=kill_col: e.tensor_scalar(
                    out=b.ap(o_sc + 4, 1), in0=b.ap(kill_col, 1),
                    scalar1=b.ap(o_sc + 2, 1), scalar2=1.0 / 32.0,
                    op0=ALU.subtract, op1=ALU.mult))
                b.drain("vector")
                vm = v_op(lambda e, i=i: e.tensor_copy(out=b.ap(o_m + i, 1),
                                                       in_=b.ap(o_sc + 2, 1)))
                b.drain("vector")
                b.wait("scalar", "vv", vm)
                af = a_op(lambda e: e.activation(out=b.ap(o_sc + 5, 1),
                                                 in_=b.ap(o_sc + 3, 1),
                                                 func=AF.Exp, scale=1.0 / 32.0))
                b.wait("scalar", "mm", b.sems["mm"].val)
                ats = []
                for bi, (bank, mt, moff) in enumerate(sco):
                    if moff is not None:
                        at = a_op(lambda e, bi=bi: e.activation(
                            out=b.ap(o_probs + bi * 512, 512), in_=b.ap(o_sm, 512),
                            func=AF.Exp, bias=b.ap(o_sc + 4, 1), scale=1.0 / 32.0))
                    else:
                        at = a_op(lambda e, bank=bank, bi=bi: e.activation(
                            out=b.ap(o_probs + bi * 512, 512), in_=ps[bank][:, 0:512],
                            func=AF.Exp, bias=b.ap(o_sc + 4, 1), scale=1.0 / 32.0))
                    ats.append(at)
                    pm.done(bank, "aa", at)
                flash_aa[0] = ats[-1]
                b.wait("vector", "aa", ats[-1])
                v_op(lambda e, nb=nb: e.tensor_reduce(
                    out=b.ap(o_sc + 6, 1), in_=b.ap(o_probs, nb * 512, F32),
                    axis=AX.X, op=ALU.add))
                b.drain("vector")
                vs_ = v_op(lambda e, i=i: e.scalar_tensor_tensor(
                    out=b.ap(o_sumc + i, 1), in0=b.ap(o_sumc + i, 1),
                    scalar=b.ap(o_sc + 5, 1), in1=b.ap(o_sc + 6, 1),
                    op0=ALU.mult, op1=ALU.add))
                b.drain("vector")
                b.wait("tensor", "aa", ats[-1])
                pm.use(2); pm.use(3)
                for bi in range(nb):
                    slot = blocks[bi][0]
                    mtT, etT = transpose_affine(o_probs + bi * 512, o_pt, nch=4,
                                                banks=(6, 7), evac_eng="scalar")
                    for cc in range(4):
                        b.wait("tensor", "aa", etT)
                        for hh in range(2):
                            bank = 2 + hh
                            last = (bi == nb - 1 and cc == 3)
                            t_op(lambda e, cc=cc, hh=hh, slot=slot, bank=bank, bi=bi, last=last:
                                 e.matmul(ps[bank][:, 0:512],
                                          b.ap(o_pt + cc * 128, 128),
                                          b.ap(o_V[slot] + cc * 1024 + hh * 512, 512),
                                          start=(bi == 0 and cc == 0), stop=last))
                b.wait("vector", "aa", ats[-1])
                b.wait("vector", "mm", b.sems["mm"].val)
                vv1 = None
                for hh in range(2):
                    vv1 = v_op(lambda e, i=i, hh=hh: e.scalar_tensor_tensor(
                        out=b.ap(o_O + i * 1024 + hh * 512, 512),
                        in0=b.ap(o_O + i * 1024 + hh * 512, 512),
                        scalar=b.ap(o_sc + 5, 1), in1=ps[2 + hh][:, 0:512],
                        op0=ALU.mult, op1=ALU.add))
                pm.done(2, "vv", vv1); pm.done(3, "vv", vv1)
                b.drain("vector")
            return kv_dep

        kb = s * 1024
        vb = 2048 + s * 1024
        lt0 = load_kv_block(0, kb, vb, 0, ag3_in)
        lt1 = load_kv_block(1, kb, vb + 512, 512, ag3_in)
        kz = b.sems["cc"].val
        def dmask_of(i):
            return o_dmask + (i % 4) * 512
        visit([(0, lt0, dmask_of), (1, lt1, dmask_of)],
              [(i, 1) for i in range(4)], o_cvals + 5, True)
        visit([(0, lt0, lambda i: None), (1, lt1, dmask_of)],
              [(i, 2) for i in range(4, 8)], o_cvals + 5, True)
        kv_dep_last = [("mm", b.sems["mm"].val), ("mm", b.sems["mm"].val)]

        b.wait("sync", "cc", cc3)
        for v in range(8):
            rho, half = v // 2, v % 2
            kb = rho * 4096 + half * 1024
            vb = rho * 4096 + 2048 + half * 1024
            kcol = o_killc + s * 8 + v
            b.wait("sync", "mm", b.sems["mm"].val)
            lt0 = load_kv_block(0, kb, vb, 0, ag3_out)
            b.wait("sync", "mm", b.sems["mm"].val)
            lt1 = load_kv_block(1, kb, vb + 512, 512, ag3_out)
            visit([(0, lt0, lambda i: None), (1, lt1, lambda i: None)],
                  [(i, 2) for i in range(8)], kcol, False)

        b.pop()
        b.barrier()
        b.push()
        o_Wo = b.sbr(NCH * 1024, "Wo")
        o_biasB = b.sbr(1024, "biasB")
        o_osc = b.sbr(1024)
        o_t2h = b.sbr(1024)
        load_biases(["out_b"], o_biasB, 1024, smW)
        ldWo = load_weight_fm(W["Wo"], 1024, o_Wo, smW)
        b.wait("tensor", smW, ldWo)
        osc_dep = [None]
        for i in range(8):
            gi = s * 8 + i
            sl = i % 2
            vr = v_op(lambda e, i=i: e.reciprocal(out=b.ap(o_sc + 8, 1),
                                                  in_=b.ap(o_sumc + i, 1)))
            b.drain("vector")
            b.wait_dep("vector", osc_dep[0])
            vo = v_op(lambda e, i=i: e.tensor_scalar(
                out=b.ap(o_osc, 1024), in0=b.ap(o_O + i * 1024, 1024),
                scalar1=b.ap(o_sc + 8, 1), scalar2=None, op0=ALU.mult))
            b.wait("tensor", "vv", vo)
            mtT, etT = transpose_affine(o_osc, o_t2h, banks=(6, 7))
            b.wait("tensor", "vv", etT)
            prs = proj_tm(o_t2h, o_Wo, 1024, bias="out_b", banks=(4, 5))
            b.wait("sync", "vv", b.sems["vv"].val)
            ltu = b.dma("sync", smLu, b.ap(o_uld, 1024), u_buf[gi * 128:(gi + 1) * 128, :])
            ltx = b.dma("sync", smLx, b.ap(o_xld, 1024), x_loc[gi * 128:(gi + 1) * 128, :])
            ev2 = None
            for (bank, h, wdt, mt) in prs:
                b.wait("vector", "mm", mt)
                b.wait("vector", smLu, ltu)
                ev2 = v_op(lambda e, bank=bank, h=h, wdt=wdt: e.tensor_tensor(
                    out=b.ap(o_x2 + h, wdt), in0=ps[bank][:, 0:wdt],
                    in1=b.ap(o_uld + h, wdt), op=ALU.add))
                pm.done(bank, "vv", ev2)
            b.drain("vector")
            b.wait("vector", smLx, ltx)
            vr2 = v_op(lambda e: e.tensor_tensor(out=b.ap(o_r3, 1024),
                                                 in0=b.ap(o_x2, 1024),
                                                 in1=b.ap(o_xld, 1024), op=ALU.add))
            b.wait("gpsimd", "vv", vr2)
            str_ = b.dma("gpsimd", smPo, r_buf[gi * 128:(gi + 1) * 128, :], b.ap(o_r3, 1024))
            _, at2 = ln_stats(o_x2, 1024, o_rsB, o_stB)
            b.wait("vector", "mm", b.sems["mm"].val)
            b.wait("vector", "aa", at2)
            th = v_op(lambda e: e.tensor_scalar(
                out=b.ap(o_t2h, 1024), in0=b.ap(o_x2, 1024),
                scalar1=b.ap(o_stB, 1), scalar2=b.ap(o_rsB, 1),
                op0=ALU.subtract, op1=ALU.mult))
            b.wait("tensor", "vv", th)
            mtT, etT = transpose_affine(o_t2h, o_osc, g_off=o_g2, bi_off=o_b2)
            b.wait("gpsimd", "vv", etT)
            sth = b.dma("gpsimd", smPo,
                        hT_buf.ap().rearrange("c p t -> p c t")[:, :, gi * 128:(gi + 1) * 128],
                        b.ap(o_osc, 1024))
            osc_dep[0] = (smPo, sth)
        b.pop()
        b.barrier()
        return None

      flash_seg(0)
      flash_seg(1)
      b.pop()
      b.barrier()

    if "M" in phases:
      smHT = b.sem("smHT"); smWa = b.sem("smWa"); smWb = b.sem("smWb")
      smAspill = b.sem("smAsp"); smAld = b.sem("smAld")
      smRld = [b.sem("smRld0"), b.sem("smRld1")]; smXsp = b.sem("smXsp")

      def mlp(inT_buf, w1, b1n, w2, b2n, resid_buf, out_buf, act="gelu"):
        b.push()
        o_w1 = [b.sbr(NCH * 1024, "w1a"), b.sbr(NCH * 1024, "w1b")]
        o_bias1 = b.sbr(4160, "biasM")
        o_hTp = b.sbr(2048)
        o_aout = b.sbr(2048)
        bt1 = load_biases([b1n], o_bias1, 4160, smW)
        b.wait("tensor", smW, bt1)
        boff1, _, brow1 = bias_off[b1n]
        hT_dep = [None]
        a_spill_dep = [None]
        for sg in range(4):
            wsl = sg % 2
            smw = smWa if wsl == 0 else smWb
            b.wait("sync", "mm", b.sems["mm"].val)
            ldw = None
            for ci in range(NCH):
                ldw = b.dma("sync", smw, b.ap(o_w1[wsl] + ci * 1024, 1024),
                            w1[ci * 128:(ci + 1) * 128, sg * 1024:(sg + 1) * 1024])
            b.wait("tensor", smw, ldw)
            for p in range(8):
                b.wait_dep("sync", hT_dep[0])
                lth = b.dma("sync", smHT, b.ap(o_hTp, 2048),
                            inT_buf.ap().rearrange("c p t -> p c t")[:, :, 256 * p:256 * (p + 1)])
                b.wait("tensor", smHT, lth)
                hT_dep[0] = ("mm", 0)
                for co in range(NCH):
                    bank = (co % 2)
                    pm.use(bank)
                    mt = None
                    for ci in range(NCH):
                        mt = t_op(lambda e, ci=ci, co=co, bank=bank, wsl=wsl:
                                  e.matmul(ps[bank][:, 0:256],
                                           b.ap(o_w1[wsl] + ci * 1024 + co * 128, 128),
                                           b.ap(o_hTp, 2048).rearrange(
                                               "q (c j) -> q c j", c=NCH)[:, ci, :],
                                           start=(ci == 0), stop=False))
                    mt = t_op(lambda e, co=co, bank=bank, sg=sg:
                              e.matmul(ps[bank][:, 0:256],
                                       b.ap(boff1 + sg * 1024 + co * 128, 128,
                                            part0=brow1, parts=1),
                                       b.ap(o_ones1, 256, part0=brow1, parts=1),
                                       start=False, stop=True))
                    hT_dep[0] = ("mm", mt)
                    b.wait("scalar", "mm", mt)
                    if a_spill_dep[0] is not None and co == 0:
                        b.wait_dep("scalar", a_spill_dep[0])
                    at = a_op(lambda e, co=co, bank=bank: e.activation(
                        out=b.ap(o_aout + co * 256, 256), in_=ps[bank][:, 0:256],
                        func=(AF.Gelu if act == "gelu" else AF.Relu)))
                    pm.done(bank, "aa", at)
                b.wait("gpsimd", "aa", at)
                sta = None
                for co in range(NCH):
                    sta = b.dma("gpsimd", smAspill, a_buf[sg, co, :, 256 * p:256 * (p + 1)],
                                b.ap(o_aout + co * 256, 256))
                a_spill_dep[0] = (smAspill, sta)
        b.pop()
        b.barrier()
        b.push()
        o_w2 = b.sbr(16 * 1024, "w2h")
        o_at = [b.sbr(4096), b.sbr(4096)]
        o_res = [b.sb(1024), b.sb(1024)]
        o_xo = b.sb(512)
        o_bias2 = b.sbr(1040, "biasM2")
        smAt = [b.sem("smAt0"), b.sem("smAt1")]
        bt2 = load_biases([b2n], o_bias2, 1040, smW)
        b.wait("tensor", smW, bt2)
        boff2, _, brow2 = bias_off[b2n]
        at_dep = [None, None]
        res_dep = [None, None]
        xo_dep = [None]
        for hh in range(2):
            b.wait("sync", "mm", b.sems["mm"].val)
            ldw = None
            for sc in range(32):
                ldw = b.dma("sync", smWa, b.ap(o_w2 + sc * 512, 512),
                            w2[sc * 128:(sc + 1) * 128, hh * 512:(hh + 1) * 512])
            b.wait("tensor", smWa, ldw)
            for i in range(NT):
                sl = i % 2
                b.wait_dep("sync", at_dep[sl])
                lta = b.dma("sync", smAt[sl], b.ap(o_at[sl], 4096),
                            a_buf.ap().rearrange("s c p t -> p (s c) t")[:, :, i * 128:(i + 1) * 128])
                b.wait_dep("sync", res_dep[sl])
                ltr = b.dma("sync", smRld[sl], b.ap(o_res[sl], 1024),
                            resid_buf[i * 128:(i + 1) * 128, :])
                b.wait("tensor", smAt[sl], lta)
                bank = 2 + sl
                pm.use(bank)
                mt = None
                for sc in range(32):
                    mt = t_op(lambda e, sc=sc, sl=sl, bank=bank:
                              e.matmul(ps[bank][:, 0:512],
                                       b.ap(o_at[sl] + sc * 128, 128),
                                       b.ap(o_w2 + sc * 512, 512),
                                       start=(sc == 0), stop=False))
                mt = t_op(lambda e, bank=bank, hh=hh:
                          e.matmul(ps[bank][:, 0:512],
                                   b.ap(o_ones1, 128, part0=brow2, parts=1),
                                   b.ap(boff2 + hh * 512, 512, part0=brow2, parts=1),
                                   start=False, stop=True))
                at_dep[sl] = ("mm", mt)
                b.wait("vector", "mm", mt)
                b.wait("vector", smRld[sl], ltr)
                b.wait_dep("vector", xo_dep[0])
                ev = v_op(lambda e, sl=sl, bank=bank, hh=hh: e.tensor_tensor(
                    out=b.ap(o_xo, 512), in0=ps[bank][:, 0:512],
                    in1=b.ap(o_res[sl] + hh * 512, 512), op=ALU.add))
                pm.done(bank, "vv", ev)
                res_dep[sl] = ("vv", ev)
                b.wait("gpsimd", "vv", ev)
                stx = b.dma("gpsimd", smXsp,
                            out_buf[i * 128:(i + 1) * 128, hh * 512:(hh + 1) * 512],
                            b.ap(o_xo, 512))
                xo_dep[0] = (smXsp, stx)
        b.pop()
        b.barrier()

      mlp(hT_buf, W["mW1"], "mb1", W["mW2"], "mb2", r_buf, x1_buf, act="gelu")

      b.push()
      o_xl = [b.sb(1024), b.sb(1024)]
      o_tf2 = b.sbr(1024)
      o_rsM = b.sb(16); o_stM = b.sb(48)
      smX1 = [b.sem("smX1a"), b.sem("smX1b")]
      xl_dep = [None, None]
      t2_dep = [None]
      for i in range(NT):
        sl = i % 2
        b.wait_dep("sync", xl_dep[sl])
        lt = b.dma("sync", smX1[sl], b.ap(o_xl[sl], 1024), x1_buf[i * 128:(i + 1) * 128, :])
        b.wait("vector", smX1[sl], lt)
        _, at1 = ln_stats(o_xl[sl], 1024, o_rsM, o_stM)
        b.wait_dep("vector", t2_dep[0])
        b.wait("vector", "aa", at1)
        tt = v_op(lambda e, sl=sl: e.tensor_scalar(
            out=b.ap(o_tf2, 1024, F32R), in0=b.ap(o_xl[sl], 1024),
            scalar1=b.ap(o_stM, 1), scalar2=b.ap(o_rsM, 1),
            op0=ALU.subtract, op1=ALU.mult))
        xl_dep[sl] = ("vv", tt)
        b.wait("tensor", "vv", tt)
        mt_last, et_last = transpose_affine(o_tf2, o_tf2, banks=(6, 7))
        b.wait("gpsimd", "vv", et_last)
        stt = b.dma("gpsimd", smXsp,
                    u2T_buf.ap().rearrange("c p t -> p c t")[:, :, i * 128:(i + 1) * 128],
                    b.ap(o_tf2, 1024))
        t2_dep[0] = (smXsp, stt)
      b.pop()
      b.barrier()

      mlp(u2T_buf, W["fW1"], "fb1", W["fW2"], "fb2", x1_buf, xb_buf, act="relu")

    if "C" in phases:
      b.push()
      o_x4T = b.sbr(NT * 1024, "x4T")
      b.push()
      o_xc = [b.sb(1024), b.sb(1024)]
      o_tc = b.sbr(1024)
      o_rsC = b.sb(16); o_stC = b.sb(48)
      smXb = [b.sem("smXb0"), b.sem("smXb1")]
      xc_dep = [None, None]
      tc_dep = [None]
      for i in range(NT):
        sl = i % 2
        b.wait_dep("sync", xc_dep[sl])
        lt = b.dma("sync", smXb[sl], b.ap(o_xc[sl], 1024), xb_buf[i * 128:(i + 1) * 128, :])
        b.wait("vector", smXb[sl], lt)
        _, at1 = ln_stats(o_xc[sl], 1024, o_rsC, o_stC)
        b.wait_dep("vector", tc_dep[0])
        b.wait("vector", "aa", at1)
        tt = v_op(lambda e, sl=sl: e.tensor_scalar(
            out=b.ap(o_tc, 1024, F32R), in0=b.ap(o_xc[sl], 1024),
            scalar1=b.ap(o_stC, 1), scalar2=b.ap(o_rsC, 1),
            op0=ALU.subtract, op1=ALU.mult))
        xc_dep[sl] = ("vv", tt)
        b.wait("tensor", "vv", tt)
        mt_last, _ = transpose_affine(o_tc, o_x4T + i * 1024)
        tc_dep[0] = ("mm", mt_last)
      b.pop()
      b.barrier()

      def x4_pair_rhs(p, ci, as_f32=False):
        ap_ = b.ap(o_x4T + 2 * p * 1024, 2048).rearrange(
            "q (t c j) -> q t c j", t=2, c=NCH)[:, :, ci, :]
        return ap_.bitcast(F32) if as_f32 else ap_

      smCp = b.sem("smCp")
      for nmw, obuf in (("aWq", q4_buf), ("aWk", k4_buf)):
        b.push()
        o_w = b.sbr(NCH * 512, "w_qk4")
        o_st4 = b.sbr(4 * 256)
        ldw = load_weight_fm(W[nmw], 512, o_w, smW)
        b.wait("tensor", smW, ldw)
        st_dep4 = [None]
        for p in range(8):
            ets = None
            for co in range(4):
                bank = co % 2
                pm.use(bank)
                mt = None
                for ci in range(NCH):
                    mt = t_op(lambda e, ci=ci, co=co, p=p, bank=bank:
                              e.matmul(ps[bank][:, 0:256],
                                       b.ap(o_w + ci * 512 + co * 128, 128),
                                       x4_pair_rhs(p, ci),
                                       start=(ci == 0), stop=(ci == NCH - 1)))
                b.wait("vector", "mm", mt)
                if co == 0:
                    b.wait_dep("vector", st_dep4[0])
                ets = v_op(lambda e, co=co, bank=bank: e.tensor_copy(
                    out=b.ap(o_st4 + co * 256, 256), in_=ps[bank][:, 0:256]))
                pm.done(bank, "vv", ets)
            b.wait("gpsimd", "vv", ets)
            stt = None
            for co in range(4):
                stt = b.dma("gpsimd", smCp, obuf[co, :, 256 * p:256 * (p + 1)],
                            b.ap(o_st4 + co * 256, 256))
            st_dep4[0] = (smCp, stt)
        b.pop()
        b.barrier()

      smVg = [b.sem("smVg0"), b.sem("smVg1")]
      for nmw, obuf, silu in (("aWv", v4_buf, False), ("aWg", g4_buf, True)):
        b.push()
        o_w = b.sbr(NCH * 1024, "w_vg4")
        o_vo = [b.sbr(1024), b.sbr(1024)]
        ldw = load_weight_fm(W[nmw], 1024, o_w, smW)
        b.wait("tensor", smW, ldw)
        vo_dep = [None, None]
        for i in range(NT):
            sl = i % 2
            prs = proj_tm(o_x4T + i * 1024, o_w, 1024, banks=(0, 1))
            eng = "scalar" if silu else "vector"
            b.wait_dep(eng, vo_dep[sl])
            et = None
            for (bank, h, wdt, mt) in prs:
                b.wait(eng, "mm", mt)
                if silu:
                    et = a_op(lambda e, bank=bank, h=h, wdt=wdt, sl=sl: e.activation(
                        out=b.ap(o_vo[sl] + h, wdt), in_=ps[bank][:, 0:wdt],
                        func=AF.Silu))
                    pm.done(bank, "aa", et)
                else:
                    et = v_op(lambda e, bank=bank, h=h, wdt=wdt, sl=sl: e.tensor_copy(
                        out=b.ap(o_vo[sl] + h, wdt), in_=ps[bank][:, 0:wdt]))
                    pm.done(bank, "vv", et)
            b.wait("gpsimd", "aa" if silu else "vv", et)
            stt = b.dma("gpsimd", smVg[sl], obuf[i * 128:(i + 1) * 128, :],
                        b.ap(o_vo[sl], 1024))
            vo_dep[sl] = (smVg[sl], stt)
        b.pop()
        b.barrier()

      b.push()
      o_wg1 = b.sbr(NCH * 16, "wgk1")
      o_wg2 = b.sbr(4 * 128, "wgk2")
      o_biasC = b.sbr(528, "biasC")
      o_gk1 = b.sbr(256)
      o_gsp = b.sb(1024)
      o_gd = b.sb(1024)
      load_biases(["bgk2"], o_biasC, 528, smW)
      b.wait("sync", "mm", b.sems["mm"].val)
      ldw = None
      for ci in range(NCH):
          ldw = b.dma("sync", smW, b.ap(o_wg1 + ci * 16, 16),
                      W["aWgk1"][ci * 128:(ci + 1) * 128, :])
      for co in range(4):
          ldw = b.dma("sync", smW, b.ap(o_wg2 + co * 128, 128, part0=0, parts=16),
                      W["aWgk2"][:, co * 128:(co + 1) * 128])
      b.wait("tensor", smW, ldw)
      boffg, _, browg = bias_off["bgk2"]
      gd_dep = [None]
      at = None
      for p in range(8):
          pm.use(4)
          mt = None
          for ci in range(NCH):
              mt = t_op(lambda e, ci=ci, p=p: e.matmul(
                  ps[4][0:16, 0:256], b.ap(o_wg1 + ci * 16, 16, F32),
                  x4_pair_rhs(p, ci, as_f32=True), start=(ci == 0), stop=(ci == NCH - 1)))
          b.wait("vector", "mm", mt)
          ev1 = v_op(lambda e: e.tensor_copy(out=b.ap(o_gk1, 256, None, 0, 16),
                                             in_=ps[4][0:16, 0:256]))
          pm.done(4, "vv", ev1)
          b.wait("tensor", "vv", ev1)
          for co in range(4):
              bank = co % 2
              pm.use(bank)
              mt2 = t_op(lambda e, co=co, bank=bank: e.matmul(
                  ps[bank][:, 0:256],
                  b.ap(o_wg2 + co * 128, 128, part0=0, parts=16),
                  b.ap(o_gk1, 256, None, 0, 16),
                  start=True, stop=False))
              mt2 = t_op(lambda e, co=co, bank=bank: e.matmul(
                  ps[bank][:, 0:256],
                  b.ap(boffg + co * 128, 128, part0=browg, parts=1),
                  b.ap(o_ones1, 256, part0=browg, parts=1),
                  start=False, stop=True))
              b.wait("scalar", "mm", mt2)
              if co == 0:
                  b.wait_dep("scalar", gd_dep[0])
              a_op(lambda e, co=co, bank=bank: e.activation(
                  out=b.ap(o_gsp + co * 256, 256), in_=ps[bank][:, 0:256],
                  func=AF.Exp, scale=-1.0))
              b.drain("scalar")
              at = a_op(lambda e, co=co: e.activation(
                  out=b.ap(o_gsp + co * 256, 256), in_=b.ap(o_gsp + co * 256, 256),
                  func=AF.Ln, bias=1.0))
              b.drain("scalar")
              pm.done(bank, "aa", at)
          b.wait("vector", "aa", at)
          vg = v_op(lambda e: e.tensor_scalar(
              out=b.ap(o_gd, 1024), in0=b.ap(o_gsp, 1024),
              scalar1=-1.0 / 16.0, scalar2=-3.0, op0=ALU.mult, op1=ALU.max))
          b.drain("vector")
          b.wait("gpsimd", "vv", vg)
          stt = None
          for co in range(4):
              stt = b.dma("gpsimd", smCp, gdec_buf[co, :, 256 * p:256 * (p + 1)],
                          b.ap(o_gd + co * 256, 256))
          gd_dep[0] = (smCp, stt)
      b.pop()
      b.pop()
      b.barrier()

      b.push()
      o_S = b.sbr(1024, "S")
      o_q4 = b.sbr(512); o_k4 = b.sbr(512)
      o_v4t = b.sbr(1024)
      o_gdt = b.sb(512)
      o_a4 = b.sb(512)
      o_ea = b.sb(512)
      o_sc4 = b.sb(512)
      o_qt4 = b.sbr(512); o_kt4 = b.sbr(512); o_kh4 = b.sbr(512)
      o_khT = b.sbr(512)
      o_pt4 = b.sbr(512)
      o_runD = b.sb(16)
      o_eD = b.sb(16)
      o_Olo = b.sb(1024)
      smC1 = b.sem("smC1")
      smC1s = b.sem("smC1s")
      c1_ld_dep = [None]
      c1_sp_dep = [None]
      zt = zero_f32r(o_S, 1024)
      b.wait("vector", "aa", zt)
      b.wait("tensor", "aa", zt)
      for i in range(NT):
        if i == 8:
            b.wait("scalar", "mm", b.sems["mm"].val)
            b.wait("scalar", "vv", b.sems["vv"].val)
            b.wait("scalar", smC1s, b.sems["smC1s"].val)
            zt = zero_f32r(o_S, 1024)
            b.wait("vector", "aa", zt)
            b.wait("tensor", "aa", zt)
        if i % 8 == 0:
            v_op(lambda e: e.memset(b.ap(o_runD, 4), 0.0))
            b.drain("vector")
        b.wait_dep("sync", c1_ld_dep[0])
        lt1 = b.dma("sync", smC1, b.ap(o_q4, 512),
                    q4_buf.ap().rearrange("c p t -> p c t")[:, :, i * 128:(i + 1) * 128])
        lt2 = b.dma("sync", smC1, b.ap(o_k4, 512),
                    k4_buf.ap().rearrange("c p t -> p c t")[:, :, i * 128:(i + 1) * 128])
        lt3 = b.dma("sync", smC1, b.ap(o_v4t, 1024), v4_buf[i * 128:(i + 1) * 128, :])
        lt4 = b.dma("sync", smC1, b.ap(o_gdt, 512),
                    gdec_buf.ap().rearrange("c p t -> p c t")[:, :, i * 128:(i + 1) * 128])
        b.wait("vector", smC1, lt4)
        v_op(lambda e, i=i: e.tensor_copy(out=b.ap(o_Dpre + i * 4, 4),
                                          in_=b.ap(o_runD, 4)))
        b.drain("vector")
        for co in range(4):
            v_op(lambda e, co=co: e.tensor_tensor_scan(
                out=b.ap(o_a4 + co * 128, 128), data0=b.ap(o_gdt + co * 128, 128),
                data1=b.ap(o_gdt + co * 128, 128), initial=0.0,
                op0=ALU.add, op1=ALU.bypass))
        b.drain("vector")
        for co in range(4):
            v_op(lambda e, co=co: e.tensor_scalar(
                out=b.ap(o_runD + co, 1), in0=b.ap(o_a4 + co * 128 + 127, 1),
                scalar1=b.ap(o_runD + co, 1), scalar2=None, op0=ALU.add))
        b.drain("vector")
        va = b.sems["vv"].val
        b.wait("scalar", "vv", va)
        aeD = a_op(lambda e: e.activation(
            out=b.ap(o_eD, 4),
            in_=b.ap(o_a4, 512).rearrange("p (c j) -> p c j", j=128)[:, :, 127],
            func=AF.Exp))
        ae = a_op(lambda e: e.activation(out=b.ap(o_ea, 512), in_=b.ap(o_a4, 512),
                                         func=AF.Exp))
        b.wait("vector", smC1, lt4)
        b.wait("vector", "aa", ae)
        b.wait_dep("vector", c1_sp_dep[0])
        vq = v_op(lambda e: e.tensor_tensor(out=b.ap(o_qt4, 512),
                                            in0=b.ap(o_q4, 512, F32),
                                            in1=b.ap(o_ea, 512), op=ALU.mult))
        v_op(lambda e: e.tensor_scalar(out=b.ap(o_sc4, 512), in0=b.ap(o_a4, 512),
                                       scalar1=-1.0, scalar2=60.0,
                                       op0=ALU.mult, op1=ALU.min))
        b.drain("vector")
        vna = b.sems["vv"].val
        b.wait("scalar", "vv", vna)
        ana = a_op(lambda e: e.activation(out=b.ap(o_sc4, 512), in_=b.ap(o_sc4, 512),
                                          func=AF.Exp))
        b.wait("vector", "aa", ana)
        b.wait("vector", smC1, lt4)
        vk = v_op(lambda e: e.tensor_tensor(out=b.ap(o_kt4, 512),
                                            in0=b.ap(o_k4, 512, F32),
                                            in1=b.ap(o_sc4, 512), op=ALU.mult))
        b.drain("vector")
        b.wait("vector", "aa", aeD)
        vkh = None
        for co in range(4):
            vkh = v_op(lambda e, co=co: e.tensor_scalar(
                out=b.ap(o_kh4 + co * 128, 128), in0=b.ap(o_kt4 + co * 128, 128, F32),
                scalar1=b.ap(o_eD + co, 1), scalar2=None, op0=ALU.mult))
        b.drain("vector")
        qkv_ready = b.sems["vv"].val
        b.wait("tensor", "vv", qkv_ready)
        b.wait("tensor", smC1, lt4)
        pm.use(4)
        mtp = None
        for co in range(4):
            mtp = t_op(lambda e, co=co: e.matmul(
                ps[4][:, co * 128:(co + 1) * 128],
                b.ap(o_kt4 + co * 128, 128),
                b.ap(o_qt4 + co * 128, 128),
                start=(co == 0), stop=(co == 3)))
        b.wait("vector", "mm", mtp)
        vpt = None
        for co in range(4):
            vpt = v_op(lambda e, co=co: e.tensor_tensor(
                out=b.ap(o_pt4 + co * 128, 128), in0=ps[4][:, co * 128:(co + 1) * 128],
                in1=b.ap(o_tri01, 128), op=ALU.mult))
        pm.done(4, "vv", vpt)
        b.drain("vector")
        b.wait("tensor", "vv", vpt)
        pm.use(0); pm.use(1)
        mo = None
        for co in range(4):
            bank = co // 2
            col = (co % 2) * 256
            mo = t_op(lambda e, co=co, bank=bank, col=col: e.matmul(
                ps[bank][:, col:col + 256],
                b.ap(o_pt4 + co * 128, 128),
                b.ap(o_v4t, 1024).rearrange("p (c j) -> p c j", c=4)[:, co, :],
                start=(co % 2 == 0), stop=False))
            mo = t_op(lambda e, co=co, bank=bank, col=col: e.matmul(
                ps[bank][:, col:col + 256],
                b.ap(o_qt4 + co * 128, 128),
                b.ap(o_S + co * 256, 256),
                start=False, stop=(co % 2 == 1)))
        b.wait("vector", "mm", mo)
        vO = v_op(lambda e: e.tensor_copy(out=b.ap(o_Olo, 512), in_=ps[0][:, 0:512]))
        vO = v_op(lambda e: e.tensor_copy(out=b.ap(o_Olo + 512, 512), in_=ps[1][:, 0:512]))
        pm.done(0, "vv", vO); pm.done(1, "vv", vO)
        b.wait("gpsimd", "vv", vO)
        sto = b.dma("gpsimd", smC1s, oraw_buf[i * 128:(i + 1) * 128, :], b.ap(o_Olo, 1024))
        b.wait("gpsimd", "vv", qkv_ready)
        stq = None
        for co in range(4):
            stq = b.dma("gpsimd", smC1s, qt_buf[co, :, i * 128:(i + 1) * 128],
                        b.ap(o_qt4 + co * 128, 128))
        c1_sp_dep[0] = (smC1s, stq)
        mtt, ett = transpose_affine(o_kh4, o_khT, nch=4, banks=(6, 7), evac_eng="scalar")
        b.wait("tensor", "aa", ett)
        pm.use(2); pm.use(3)
        mds = None
        for co in range(4):
            bank = 2 + co // 2
            col = (co % 2) * 256
            mds = t_op(lambda e, co=co, bank=bank, col=col: e.matmul(
                ps[bank][:, col:col + 256],
                b.ap(o_khT + co * 128, 128),
                b.ap(o_v4t, 1024).rearrange("p (c j) -> p c j", c=4)[:, co, :],
                start=(co % 2 == 0), stop=(co % 2 == 1)))
        b.wait("vector", "mm", mds)
        vS = None
        for co in range(4):
            bank = 2 + co // 2
            col = (co % 2) * 256
            vS = v_op(lambda e, co=co, bank=bank, col=col: e.scalar_tensor_tensor(
                out=b.ap(o_S + co * 256, 256), in0=b.ap(o_S + co * 256, 256, F32),
                scalar=b.ap(o_eD + co, 1), in1=ps[bank][:, col:col + 256],
                op0=ALU.mult, op1=ALU.add))
        pm.done(2, "vv", vS); pm.done(3, "vv", vS)
        b.drain("vector")
        c1_ld_dep[0] = ("mm", b.sems["mm"].val)
        if i % 8 == 7:
            seg = i // 8
            v_op(lambda e, seg=seg: e.tensor_copy(
                out=b.ap(o_Dseg + seg * 8, 4), in_=b.ap(o_runD, 4)))
            b.drain("vector")
            b.wait("gpsimd", "vv", b.sems["vv"].val)
            b.flush("gpsimd", smC1s)
            b.dma("gpsimd", smC1s, ag4_in[1 + seg * 128: 129 + seg * 128, :],
                  b.ap(o_S, 1024, F32))
            b.dma("gpsimd", smC1s,
                  ag4_in.ap()[0:1, seg * 512:(seg + 1) * 512].rearrange(
                      "one (c p) -> (one p) c", p=128)[:, 0:4],
                  b.ap(o_Dseg + seg * 8, 4), slow=True)
      b.pop()
      b.barrier()

      cc4 = b.op("gpsimd", lambda e: e.collective_compute(
          "AllGather", ALU.bypass, replica_groups=groups,
          ins=[ag4_in.ap()], outs=[ag4_out.ap()]), inc="cc")

      b.push()
      o_Sin = b.sbr(2 * 1024, "Sin")
      b.push()
      o_Dsl = b.sb(32)
      o_prefA = b.sb(16); o_prefB = b.sb(16)
      o_wj = b.sb(32)
      o_Sj = b.sb(1024)
      o_SinP = [b.sbr(1024), b.sbr(1024)]
      o_msk = b.sb(32)
      smC2 = b.sem("smC2")
      b.wait("sync", "cc", b.sems["cc"].val)
      lt = None
      for j in range(8):
          rank = j if j <= 3 else 7 - j
          half = 0 if j <= 3 else 1
          lt = b.dma("sync", smC2,
                     b.ap(o_Dsl, 32).rearrange("p (c j) -> p c j", j=8)[:, :, j],
                     ag4_out.ap()[rank * 257: rank * 257 + 1,
                                  half * 512:(half + 1) * 512].rearrange(
                         "one (c p) -> (one p) c", p=128)[:, 0:4], slow=True)
      b.flush("sync", smC2)
      b.wait("vector", smC2, lt)
      v_op(lambda e: e.tensor_tensor(out=b.ap(o_wj, 32), in0=b.ap(o_Dsl, 32),
                                     in1=b.ap(o_maskA4, 32), op=ALU.mult))
      b.drain("vector")
      v_op(lambda e: e.tensor_reduce(
          out=b.ap(o_prefA, 4),
          in_=b.ap(o_wj, 32).rearrange("p (c j) -> p c j", j=8), axis=AX.X, op=ALU.add))
      b.drain("vector")
      v_op(lambda e: e.tensor_tensor(out=b.ap(o_wj, 32), in0=b.ap(o_Dsl, 32),
                                     in1=b.ap(o_maskB4, 32), op=ALU.mult))
      b.drain("vector")
      v_op(lambda e: e.tensor_reduce(
          out=b.ap(o_prefB, 4),
          in_=b.ap(o_wj, 32).rearrange("p (c j) -> p c j", j=8), axis=AX.X, op=ALU.add))
      b.drain("vector")
      for j in range(1, 8):
          v_op(lambda e, j=j: e.tensor_tensor(
              out=b.ap(o_Dsl, 32).rearrange("p (c j) -> p c j", j=8)[:, :, j],
              in0=b.ap(o_Dsl, 32).rearrange("p (c j) -> p c j", j=8)[:, :, j],
              in1=b.ap(o_Dsl, 32).rearrange("p (c j) -> p c j", j=8)[:, :, j - 1],
              op=ALU.add))
          b.drain("vector")
      sin_final = None
      for tix, (opref, omsk) in enumerate(((o_prefA, o_maskA4), (o_prefB, o_maskB4))):
          b.wait("scalar", "vv", b.sems["vv"].val)
          mzt = zero_f32r(o_SinP[0], 1024)
          b.wait("vector", "aa", mzt)
          cur = 0
          for j in range(8):
              v_op(lambda e, j=j, opref=opref: e.tensor_scalar(
                  out=b.ap(o_wj, 4),
                  in0=b.ap(o_Dsl, 32).rearrange("p (c j) -> p c j", j=8)[:, :, j],
                  scalar1=-1.0, scalar2=None, op0=ALU.mult))
              b.drain("vector")
              v_op(lambda e, opref=opref: e.tensor_tensor(
                  out=b.ap(o_wj, 4), in0=b.ap(o_wj, 4),
                  in1=b.ap(opref, 4), op=ALU.add))
              b.drain("vector")
              v_op(lambda e: e.tensor_scalar(
                  out=b.ap(o_wj, 4), in0=b.ap(o_wj, 4),
                  scalar1=0.0, scalar2=None, op0=ALU.min))
              b.drain("vector")
              vw = b.sems["vv"].val
              b.wait("scalar", "vv", vw)
              aw = a_op(lambda e: e.activation(out=b.ap(o_wj, 4), in_=b.ap(o_wj, 4),
                                               func=AF.Exp))
              b.wait("vector", "aa", aw)
              vwm = v_op(lambda e, j=j, omsk=omsk: e.tensor_tensor(
                  out=b.ap(o_wj + 8, 4), in0=b.ap(o_wj, 4),
                  in1=b.ap(omsk, 32).rearrange("p (c j) -> p c j", j=8)[:, :, j],
                  op=ALU.mult))
              b.drain("vector")
              rank = j if j <= 3 else 7 - j
              half = 0 if j <= 3 else 1
              b.wait("sync", "vv", b.sems["vv"].val)
              ltS = b.dma("sync", smC2, b.ap(o_Sj, 1024),
                          ag4_out[rank * 257 + 1 + half * 128: rank * 257 + 129 + half * 128, :])
              b.wait("vector", smC2, ltS)
              vacc = None
              for co in range(4):
                  vacc = v_op(lambda e, co=co, j=j, cur=cur: e.scalar_tensor_tensor(
                      out=b.ap(o_SinP[1 - cur] + co * 256, 256),
                      in0=b.ap(o_Sj + co * 256, 256),
                      scalar=b.ap(o_wj + 8 + co, 1),
                      in1=b.ap(o_SinP[cur] + co * 256, 256, F32),
                      op0=ALU.mult, op1=ALU.add))
              b.drain("vector")
              cur = 1 - cur
          vfin = v_op(lambda e, tix=tix, cur=cur: e.tensor_copy(
              out=b.ap(o_Sin + tix * 1024, 1024),
              in_=b.ap(o_SinP[cur], 1024, F32)))
          b.drain("vector")
          sin_final = vfin
      sin_ready = b.sems["vv"].val
      if "sin" in dbg_names:
          t = dbg("sin", [2, 128, 1024])
          b.wait("gpsimd", "vv", sin_ready)
          b.dma("gpsimd", smC2, t[0], b.ap(o_Sin, 1024, F32))
          b.dma("gpsimd", smC2, t[1], b.ap(o_Sin + 1024, 1024, F32))
      if "ag4" in dbg_names:
          t = dbg("ag4", [4 * 258, 1024])
          b.dma("gpsimd", smC2, t.ap(), ag4_out.ap())
      b.pop()
      b.barrier()

      b.push()
      o_Wout = b.sbr(NCH * 1024, "Wout")
      o_or3 = b.sb(1024)
      o_qt3 = b.sbr(512)
      o_q3 = b.sbr(512)
      o_eP = b.sb(16)
      o_rs4 = b.sb(16)
      o_msq = b.sb(16)
      o_Om = b.sb(1024)
      o_scrC = b.sb(256)
      o_g4t = b.sb(1024)
      o_xbt = b.sb(1024)
      o_on = b.sbr(1024)
      o_og = b.sbr(1024)
      o_ogT = b.sbr(1024)
      o_x3o = b.sb(1024)
      smC3 = b.sem("smC3")
      smC3s = b.sem("smC3s")
      ldwo = load_weight_fm(W["aWout"], 1024, o_Wout, smW)
      b.wait("tensor", smW, ldwo)
      c3_dep = [None]
      c3_sp = [None]
      for i in range(NT):
        seg = i // 8
        b.wait_dep("sync", c3_dep[0])
        lt1 = b.dma("sync", smC3, b.ap(o_or3, 1024), oraw_buf[i * 128:(i + 1) * 128, :])
        lt2 = b.dma("sync", smC3, b.ap(o_qt3, 512),
                    qt_buf.ap().rearrange("c p t -> p c t")[:, :, i * 128:(i + 1) * 128])
        lt3 = b.dma("sync", smC3, b.ap(o_g4t, 1024), g4_buf[i * 128:(i + 1) * 128, :])
        lt4 = b.dma("sync", smC3, b.ap(o_xbt, 1024), xb_buf[i * 128:(i + 1) * 128, :])
        aP = a_op(lambda e, i=i: e.activation(out=b.ap(o_eP, 4),
                                              in_=b.ap(o_Dpre + i * 4, 4), func=AF.Exp))
        b.wait("vector", "aa", aP)
        b.wait("vector", smC3, lt4)
        b.wait("vector", "vv", sin_ready)
        vq3 = None
        for co in range(4):
            vq3 = v_op(lambda e, co=co: e.tensor_scalar(
                out=b.ap(o_q3 + co * 128, 128), in0=b.ap(o_qt3 + co * 128, 128, F32),
                scalar1=b.ap(o_eP + co, 1), scalar2=None, op0=ALU.mult))
        b.drain("vector")
        b.wait("tensor", "vv", b.sems["vv"].val)
        pm.use(0); pm.use(1)
        mo = None
        for co in range(4):
            bank = co // 2
            col = (co % 2) * 256
            mo = t_op(lambda e, co=co, bank=bank, col=col, seg=seg: e.matmul(
                ps[bank][:, col:col + 256],
                b.ap(o_q3 + co * 128, 128),
                b.ap(o_Sin + seg * 1024 + co * 256, 256),
                start=(co % 2 == 0), stop=(co % 2 == 1)))
        b.wait("vector", "mm", mo)
        b.wait("vector", smC3, lt4)
        vm1 = v_op(lambda e: e.tensor_tensor(out=b.ap(o_Om, 512), in0=ps[0][:, 0:512],
                                             in1=b.ap(o_or3, 512), op=ALU.add))
        vm2 = v_op(lambda e: e.tensor_tensor(out=b.ap(o_Om + 512, 512), in0=ps[1][:, 0:512],
                                             in1=b.ap(o_or3 + 512, 512), op=ALU.add))
        pm.done(0, "vv", vm2); pm.done(1, "vv", vm2)
        b.drain("vector")
        vOm = b.sems["vv"].val
        b.wait("scalar", "vv", vOm)
        for co in range(4):
            a_op(lambda e, co=co: e.activation(
                out=b.ap(o_scrC, 256), in_=b.ap(o_Om + co * 256, 256),
                func=AF.Square, accum_out=b.ap(o_msq + co, 1)))
            b.drain("scalar")
        a_op(lambda e: e.activation(out=b.ap(o_rs4, 4), in_=b.ap(o_msq, 4),
                                    func=AF.Ln, bias=LN_EPS, scale=1.0 / 256.0))
        b.drain("scalar")
        ars = a_op(lambda e: e.activation(out=b.ap(o_rs4, 4), in_=b.ap(o_rs4, 4),
                                          func=AF.Exp, scale=-0.5))
        b.wait("vector", "aa", ars)
        von = None
        for co in range(4):
            von = v_op(lambda e, co=co: e.scalar_tensor_tensor(
                out=b.ap(o_on + co * 256, 256), in0=b.ap(o_Om + co * 256, 256),
                scalar=b.ap(o_rs4 + co, 1), in1=b.ap(o_rmsw, 256),
                op0=ALU.mult, op1=ALU.mult))
        b.drain("vector")
        b.wait("vector", smC3, lt4)
        vog = v_op(lambda e: e.tensor_tensor(out=b.ap(o_og, 1024),
                                             in0=b.ap(o_on, 1024, F32),
                                             in1=b.ap(o_g4t, 1024), op=ALU.mult))
        b.wait("tensor", "vv", vog)
        mtT, etT = transpose_affine(o_og, o_ogT)
        b.wait("tensor", "vv", etT)
        prs = proj_tm(o_ogT, o_Wout, 1024, banks=(4, 5))
        ev3 = None
        b.wait_dep("vector", c3_sp[0])
        for (bank, h, wdt, mt) in prs:
            b.wait("vector", "mm", mt)
            ev3 = v_op(lambda e, bank=bank, h=h, wdt=wdt: e.tensor_tensor(
                out=b.ap(o_x3o + h, wdt), in0=ps[bank][:, 0:wdt],
                in1=b.ap(o_xbt + h, wdt), op=ALU.add))
            pm.done(bank, "vv", ev3)
        c3_dep[0] = ("vv", ev3)
        b.wait("gpsimd", "vv", ev3)
        b.wait_dep("gpsimd", c3_sp[0])
        st3 = b.dma("gpsimd", smC3s, x3_buf[i * 128:(i + 1) * 128, :], b.ap(o_x3o, 1024))
        c3_sp[0] = (smC3s, st3)
      b.pop()
      b.pop()
      b.barrier()

    if "DD" in phases:
      b.push()
      o_x5 = [b.sb(1024), b.sb(1024)]
      o_u5 = b.sbr(1024)
      o_rsD = b.sb(16); o_stD = b.sb(48)
      o_mrow = b.sb(1024)
      smX3 = [b.sem("smX3a"), b.sem("smX3b")]
      smD0 = b.sem("smD0")
      x5_dep = [None, None]
      u5_dep = [None]
      pm.use(4); pm.use(5)
      for i in range(NT):
        sl = i % 2
        b.wait_dep("sync", x5_dep[sl])
        lt = b.dma("sync", smX3[sl], b.ap(o_x5[sl], 1024), x3_buf[i * 128:(i + 1) * 128, :])
        b.wait("vector", smX3[sl], lt)
        _, at1 = ln_stats(o_x5[sl], 1024, o_rsD, o_stD)
        if u5_dep[0] is not None:
            for dp in u5_dep[0]:
                b.wait_dep("vector", dp)
        b.wait("vector", "aa", at1)
        tt = v_op(lambda e, sl=sl: e.tensor_scalar(
            out=b.ap(o_u5, 1024, F32R), in0=b.ap(o_x5[sl], 1024),
            scalar1=b.ap(o_stD, 1), scalar2=b.ap(o_rsD, 1),
            op0=ALU.subtract, op1=ALU.mult))
        x5_dep[sl] = ("vv", tt)
        b.wait("gpsimd", "vv", tt)
        stt = b.dma("gpsimd", smD0, dcy_buf[0][i * 128:(i + 1) * 128, :], b.ap(o_u5, 1024))
        b.wait("tensor", "vv", tt)
        mt = None
        for hh in range(2):
            mt = t_op(lambda e, i=i, hh=hh: e.matmul(
                ps[4 + hh][0:1, 0:512],
                b.ap(o_ones1, 1),
                b.ap(o_u5 + hh * 512, 512),
                start=(i == 0), stop=(i == NT - 1)))
        u5_dep[0] = [(smD0, stt), ("mm", mt)]
      b.wait("vector", "mm", b.sems["mm"].val)
      vm1 = v_op(lambda e: e.tensor_copy(out=b.ap(o_mrow, 512, None, 0, 1),
                                         in_=ps[4][0:1, 0:512]))
      vm1 = v_op(lambda e: e.tensor_copy(out=b.ap(o_mrow + 512, 512, None, 0, 1),
                                         in_=ps[5][0:1, 0:512]))
      pm.done(4, "vv", vm1); pm.done(5, "vv", vm1)
      b.drain("vector")
      b.wait("gpsimd", "vv", vm1)
      stm = b.dma("gpsimd", smD0, ag5_in.ap(), b.ap(o_mrow, 1024, None, 0, 1))
      b.wait("gpsimd", smD0, b.sems["smD0"].val)
      cc5 = b.op("gpsimd", lambda e: e.collective_compute(
          "AllGather", ALU.bypass, replica_groups=groups,
          ins=[ag5_in.ap()], outs=[ag5_out.ap()]), inc="cc")
      b.pop()
      b.barrier()

      b.push()
      o_ctx = b.sb(64)
      o_cT = b.sbr(16)
      o_wgc1 = b.sbr(NCH * 512, "gcW1")
      o_wgc2 = b.sbr(4 * 256, "gcW2")
      o_wgc3 = b.sbr(2 * 8, "gcW3")
      o_biasD0 = b.sbr(1040, "biasGC")
      o_h1g = b.sbr(16)
      o_h2g = b.sbr(16)
      smGF = b.sem("smGF")
      load_biases(["gcb1", "gcb2", "gcb3"], o_biasD0, 1040, smW)
      ldw = load_weight_fm(W["gcW1"], 512, o_wgc1, smW)
      for ci in range(4):
          ldw = b.dma("sync", smW, b.ap(o_wgc2 + ci * 256, 256),
                      W["gcW2"][ci * 128:(ci + 1) * 128, :])
      for ci in range(2):
          ldw = b.dma("sync", smW, b.ap(o_wgc3 + ci * 8, 8),
                      W["gcW3"][ci * 128:(ci + 1) * 128, :])
      b.wait("sync", "cc", b.sems["cc"].val)
      lt = None
      for j in range(4):
          lt = b.dma("sync", smGF,
                     b.ap(o_ctx, 32).rearrange("p (c j) -> p c j", j=4)[:, :, j],
                     fm_row_ap(ag5_out, j), slow=True)
      b.flush("sync", smGF)
      b.wait("vector", smGF, lt)
      v_op(lambda e: e.tensor_reduce(
          out=b.ap(o_ctx + 32, NCH),
          in_=b.ap(o_ctx, 32).rearrange("p (c j) -> p c j", j=4), axis=AX.X, op=ALU.add))
      b.drain("vector")
      vc = v_op(lambda e: e.tensor_scalar(out=b.ap(o_cT, NCH, F32R),
                                          in0=b.ap(o_ctx + 32, NCH),
                                          scalar1=1.0 / 8192.0, scalar2=None, op0=ALU.mult))
      b.drain("vector")
      b.wait("tensor", "vv", vc)
      b.wait("tensor", smW, ldw)
      boh, _, brh = bias_off["gcb1"]
      pm.use(6)
      mt = None
      for co in range(4):
          for ci in range(NCH):
              mt = t_op(lambda e, co=co, ci=ci: e.matmul(
                  ps[6][:, co:co + 1],
                  b.ap(o_wgc1 + ci * 512 + co * 128, 128, F32),
                  b.ap(o_cT + ci, 1, F32),
                  start=(co == 0 and ci == 0), stop=False))
          mt = t_op(lambda e, co=co, boh=boh, brh=brh: e.matmul(
              ps[6][:, co:co + 1],
              b.ap(boh + co * 128, 128, F32, part0=brh, parts=1),
              b.ap(o_ones1, 1, F32, part0=brh, parts=1),
              start=False, stop=(co == 3)))
      b.wait("scalar", "mm", mt)
      ag1a = a_op(lambda e: e.activation(out=b.ap(o_h1g, 4, F32R), in_=ps[6][:, 0:4],
                                         func=AF.Gelu))
      pm.done(6, "aa", ag1a)
      b.wait("tensor", "aa", ag1a)
      boh2, _, brh2 = bias_off["gcb2"]
      pm.use(7)
      mt = None
      for co in range(2):
          for ci in range(4):
              mt = t_op(lambda e, co=co, ci=ci: e.matmul(
                  ps[7][:, co:co + 1],
                  b.ap(o_wgc2 + ci * 256 + co * 128, 128, F32),
                  b.ap(o_h1g + ci, 1, F32),
                  start=(co == 0 and ci == 0), stop=False))
          mt = t_op(lambda e, co=co, boh2=boh2, brh2=brh2: e.matmul(
              ps[7][:, co:co + 1],
              b.ap(boh2 + co * 128, 128, F32, part0=brh2, parts=1),
              b.ap(o_ones1, 1, F32, part0=brh2, parts=1),
              start=False, stop=(co == 1)))
      b.wait("scalar", "mm", mt)
      ag2a = a_op(lambda e: e.activation(out=b.ap(o_h2g, 2, F32R), in_=ps[7][:, 0:2],
                                         func=AF.Gelu))
      pm.done(7, "aa", ag2a)
      b.wait("tensor", "aa", ag2a)
      boh3, _, brh3 = bias_off["gcb3"]
      pm.use(6)
      mt = None
      for ci in range(2):
          mt = t_op(lambda e, ci=ci: e.matmul(
              ps[6][0:8, 0:1],
              b.ap(o_wgc3 + ci * 8, 8, F32),
              b.ap(o_h2g + ci, 1, F32),
              start=(ci == 0), stop=False))
      mt = t_op(lambda e, boh3=boh3, brh3=brh3: e.matmul(
          ps[6][0:8, 0:1],
          b.ap(boh3, 8, F32, part0=brh3, parts=1),
          b.ap(o_ones1, 1, F32, part0=brh3, parts=1),
          start=False, stop=True))
      b.wait("scalar", "mm", mt)
      agf = a_op(lambda e: e.activation(out=b.ap(o_gf, 1, None, 0, 1),
                                        in_=ps[6][0:1, 0:1], func=AF.Sigmoid))
      pm.done(6, "aa", agf)
      b.wait("vector", "aa", agf)
      v_op(lambda e: e.tensor_scalar(out=b.ap(o_gf + 1, 1, None, 0, 1),
                                     in0=b.ap(o_gf, 1, None, 0, 1),
                                     scalar1=0.7, scalar2=None, op0=ALU.mult))
      v_op(lambda e: e.tensor_scalar(out=b.ap(o_gf + 2, 1, None, 0, 1),
                                     in0=b.ap(o_ar, 1, None, 0, 1),
                                     scalar1=2.0, scalar2=None, op0=ALU.mult))
      gf_ready = b.sems["vv"].val
      b.drain("vector")
      b.pop()
      b.barrier()

      b.push()
      o_wc1 = b.sbr(NCH * 1024, "cnW1")
      o_wc2 = b.sbr(NCH * 512, "cnW2")
      o_wc3 = b.sbr(4 * 256, "cnW3")
      o_wc4 = b.sbr(2 * 8, "cnW4")
      o_biasD = b.sbr(1056, "biasCN")
      o_dcin = b.sbr(2048)
      o_dT = b.sbr(2048)
      o_c1 = b.sbr(2048)
      o_c2 = b.sbr(1024)
      o_c3 = b.sbr(512)
      o_tfr = b.sb(512)
      o_cplr = b.sbr(512)
      o_cpl = b.sb(16)
      o_dnx = b.sbr(2048)
      smDc = b.sem("smDc")
      smDs = b.sem("smDs")
      load_biases(["cnb1", "cnb2", "cnb3", "cnb4"], o_biasD, 1056, smW)
      ldw = load_weight_fm(W["cnW1"], 1024, o_wc1, smW)
      ldw = load_weight_fm(W["cnW2"], 512, o_wc2, smW)
      for ci in range(4):
          ldw = b.dma("sync", smW, b.ap(o_wc3 + ci * 256, 256),
                      W["cnW3"][ci * 128:(ci + 1) * 128, :])
      for ci in range(2):
          ldw = b.dma("sync", smW, b.ap(o_wc4 + ci * 8, 8),
                      W["cnW4"][ci * 128:(ci + 1) * 128, :])
      b.wait("tensor", smW, ldw)

      def dT_pair_rhs(ci):
          return b.ap(o_dT, 2048).rearrange(
              "q (t c j) -> q t c j", t=2, c=NCH)[:, :, ci, :]

      dc_dep = [None]
      ds_dep = [None]
      for cyc in range(NCYC):
        inb = dcy_buf[cyc % 2]
        outb = dcy_buf[1 - cyc % 2]
        for p in range(8):
            b.wait_dep("sync", dc_dep[0])
            ltd = b.dma("sync", smDc, b.ap(o_dcin, 2048), inb[256 * p:256 * (p + 1), :])
            b.wait("tensor", smDc, ltd)
            for t in range(2):
                mtT, etT = transpose_affine(o_dcin + t * 1024, o_dT + t * 1024)
            b.wait("tensor", "vv", etT)
            bo1, _, br1 = bias_off["cnb1"]
            ats = None
            for co in range(NCH):
                bank = co % 2
                pm.use(bank)
                mt = None
                for ci in range(NCH):
                    mt = t_op(lambda e, co=co, ci=ci, bank=bank, etT=etT: e.matmul(
                        ps[bank][:, 0:256],
                        b.ap(o_wc1 + ci * 1024 + co * 128, 128),
                        dT_pair_rhs(ci), start=(ci == 0), stop=False))
                mt = t_op(lambda e, co=co, bank=bank, bo1=bo1, br1=br1: e.matmul(
                    ps[bank][:, 0:256],
                    b.ap(bo1 + co * 128, 128, part0=br1, parts=1),
                    b.ap(o_ones1, 256, part0=br1, parts=1),
                    start=False, stop=True))
                b.wait("scalar", "mm", mt)
                ats = a_op(lambda e, co=co, bank=bank: e.activation(
                    out=b.ap(o_c1 + co * 256, 256), in_=ps[bank][:, 0:256],
                    func=AF.Gelu))
                pm.done(bank, "aa", ats)
            bo2, _, br2 = bias_off["cnb2"]
            b.wait("tensor", "aa", ats)
            for co in range(4):
                bank = 2 + co % 2
                pm.use(bank)
                mt = None
                for ci in range(NCH):
                    mt = t_op(lambda e, co=co, ci=ci, bank=bank: e.matmul(
                        ps[bank][:, 0:256],
                        b.ap(o_wc2 + ci * 512 + co * 128, 128),
                        b.ap(o_c1 + ci * 256, 256),
                        start=(ci == 0), stop=False))
                mt = t_op(lambda e, co=co, bank=bank, bo2=bo2, br2=br2: e.matmul(
                    ps[bank][:, 0:256],
                    b.ap(bo2 + co * 128, 128, part0=br2, parts=1),
                    b.ap(o_ones1, 256, part0=br2, parts=1),
                    start=False, stop=True))
                b.wait("scalar", "mm", mt)
                ats = a_op(lambda e, co=co, bank=bank: e.activation(
                    out=b.ap(o_c2 + co * 256, 256), in_=ps[bank][:, 0:256],
                    func=AF.Gelu))
                pm.done(bank, "aa", ats)
            bo3, _, br3 = bias_off["cnb3"]
            b.wait("tensor", "aa", ats)
            pm.use(4)
            for co in range(2):
                mt = None
                for ci in range(4):
                    mt = t_op(lambda e, co=co, ci=ci: e.matmul(
                        ps[4][:, co * 256:co * 256 + 256],
                        b.ap(o_wc3 + ci * 256 + co * 128, 128),
                        b.ap(o_c2 + ci * 256, 256),
                        start=(co == 0 and ci == 0), stop=False))
                mt = t_op(lambda e, co=co, bo3=bo3, br3=br3: e.matmul(
                    ps[4][:, co * 256:co * 256 + 256],
                    b.ap(bo3 + co * 128, 128, part0=br3, parts=1),
                    b.ap(o_ones1, 256, part0=br3, parts=1),
                    start=False, stop=(co == 1)))
            b.wait("scalar", "mm", mt)
            ats = a_op(lambda e: e.activation(
                out=b.ap(o_c3, 512), in_=ps[4][:, 0:512], func=AF.Gelu))
            pm.done(4, "aa", ats)
            bo4, _, br4 = bias_off["cnb4"]
            b.wait("tensor", "aa", ats)
            pm.use(5)
            mt = None
            for ci in range(2):
                mt = t_op(lambda e, ci=ci: e.matmul(
                    ps[5][0:8, 0:256],
                    b.ap(o_wc4 + ci * 8, 8, F32),
                    b.ap(o_c3 + ci * 256, 256, F32),
                    start=(ci == 0), stop=False))
            mt = t_op(lambda e, bo4=bo4, br4=br4: e.matmul(
                ps[5][0:8, 0:256],
                b.ap(bo4, 8, F32, part0=br4, parts=1),
                b.ap(o_ones1, 256, F32, part0=br4, parts=1),
                start=False, stop=True))
            b.wait("scalar", "mm", mt)
            atf = a_op(lambda e: e.activation(out=b.ap(o_tfr, 256, None, 0, 1),
                                              in_=ps[5][0:1, 0:256], func=AF.Sigmoid))
            pm.done(5, "aa", atf)
            b.wait("vector", "aa", atf)
            b.wait("vector", "vv", gf_ready)
            v_op(lambda e: e.tensor_scalar(
                out=b.ap(o_tfr + 256, 256, None, 0, 1),
                in0=b.ap(o_tfr, 256, None, 0, 1),
                scalar1=0.3, scalar2=b.ap(o_gf + 1, 1, None, 0, 1),
                op0=ALU.mult, op1=ALU.add))
            b.drain("vector")
            v_op(lambda e: e.tensor_scalar(
                out=b.ap(o_tfr + 256, 256, None, 0, 1),
                in0=b.ap(o_tfr + 256, 256, None, 0, 1),
                scalar1=0.5, scalar2=b.ap(o_gf + 2, 1, None, 0, 1),
                op0=ALU.subtract, op1=ALU.mult))
            b.drain("vector")
            vcp = v_op(lambda e: e.tensor_scalar(
                out=b.ap(o_cplr, 256, F32R, 0, 1),
                in0=b.ap(o_tfr + 256, 256, None, 0, 1),
                scalar1=0.1, scalar2=None, op0=ALU.add))
            b.drain("vector")
            b.wait("gpsimd", "vv", vcp)
            b.dma("gpsimd", smDs, clkrow_buf[0:1, 0:256], b.ap(o_cplr, 256, F32, 0, 1))
            b.flush("gpsimd", smDs)
            ltc = b.dma("gpsimd", smDs, b.ap(o_cpl, 2),
                        clkrow_buf.ap()[0:1, 0:256].rearrange(
                            "one (c p) -> (one p) c", p=128), slow=True)
            b.wait("vector", smDs, ltc)
            vcn = v_op(lambda e: e.tensor_scalar(out=b.ap(o_cpl + 2, 2),
                                                 in0=b.ap(o_cpl, 2),
                                                 scalar1=-1.0, scalar2=None, op0=ALU.mult))
            b.drain("vector")
            dc_dep[0] = (smDs, ltc)
            b.wait_dep("vector", ds_dep[0])
            vu = None
            for t in range(2):
                base = o_dcin + t * 1024
                nbase = o_dnx + t * 1024
                for (qd, qs, neg) in ((0, 2, False), (1, 3, True), (2, 0, True), (3, 1, False)):
                    vu = v_op(lambda e, qd=qd, qs=qs, neg=neg, base=base, nbase=nbase, t=t:
                              e.scalar_tensor_tensor(
                                  out=b.ap(nbase + qd * 256, 256),
                                  in0=b.ap(base + qs * 256, 256, F32),
                                  scalar=b.ap(o_cpl + (2 if neg else 0) + t, 1),
                                  in1=b.ap(base + qd * 256, 256, F32),
                                  op0=ALU.mult, op1=ALU.add))
            b.drain("vector")
            dc_dep[0] = ("vv", vu)
            b.wait("gpsimd", "vv", vu)
            b.wait_dep("gpsimd", ds_dep[0])
            std = b.dma("gpsimd", smDs, outb[256 * p:256 * (p + 1), :], b.ap(o_dnx, 2048))
            ds_dep[0] = (smDs, std)
      b.pop()
      b.barrier()

      b.push()
      o_fx = [b.sb(1024), b.sb(1024)]
      o_fo = [b.sb(1024), b.sb(1024)]
      o_fy = b.sb(1024)
      smFx = [b.sem("smFx0"), b.sem("smFx1")]
      smFo = [b.sem("smFo0"), b.sem("smFo1")]
      smY = b.sem("smY")
      fdep = [None, None]
      ydep = [None]
      final_in = dcy_buf[NCYC % 2]
      for i in range(NT):
        sl = i % 2
        b.wait_dep("sync", fdep[sl])
        b.wait_dep("gpsimd", fdep[sl])
        lt1 = b.dma("sync", smFx[sl], b.ap(o_fx[sl], 1024), x3_buf[i * 128:(i + 1) * 128, :])
        lt2 = b.dma("gpsimd", smFo[sl], b.ap(o_fo[sl], 1024),
                    final_in[i * 128:(i + 1) * 128, :])
        b.wait("vector", smFx[sl], lt1)
        b.wait("vector", smFo[sl], lt2)
        b.wait_dep("vector", ydep[0])
        vy = v_op(lambda e, sl=sl: e.tensor_tensor(
            out=b.ap(o_fy, 1024), in0=b.ap(o_fx[sl], 1024),
            in1=b.ap(o_fo[sl], 1024), op=ALU.add))
        fdep[sl] = ("vv", vy)
        b.drain("vector")
        b.wait("gpsimd", "vv", vy)
        sty = b.dma("gpsimd", smY, y_out[i * 128:(i + 1) * 128, :], b.ap(o_fy, 1024))
        ydep[0] = (smY, sty)
      b.wait("gpsimd", smY, b.sems["smY"].val)
      b.pop()

    if dbg_names:
        b.barrier()
        b.wait("gpsimd", "cc", b.sems["cc"].val)
        smDbg = b.sem("smDbg")
        for nm, buf, shape in [
            ("u", u_buf, [TOK, D]), ("q1", q1_buf, [TOK, D]), ("k1", k1_buf, [TOK, D]),
            ("G", gcum_buf, [NCH, 128, TOK]), ("pT", pT_buf, [NCH, 128, TOK]),
            ("fq", fq_buf, [NCH, 128, TOK]), ("fk", fk_buf, [NCH, 128, TOK]),
            ("pcum", penc_buf, [NCH, 128, TOK]), ("qh", qh_buf, [NCH, 128, TOK]),
            ("ag3", ag3_out, [4 * 4096, D]),
            ("r", r_buf, [TOK, D]), ("hT", hT_buf, [NCH, 128, TOK]),
            ("x1", x1_buf, [TOK, D]), ("u2T", u2T_buf, [NCH, 128, TOK]),
            ("xb", xb_buf, [TOK, D]),
            ("q4", q4_buf, [4, 128, TOK]), ("k4", k4_buf, [4, 128, TOK]),
            ("v4", v4_buf, [TOK, D]), ("g4", g4_buf, [TOK, D]),
            ("gdec", gdec_buf, [4, 128, TOK]), ("oraw", oraw_buf, [TOK, D]),
            ("qt", qt_buf, [4, 128, TOK]),
            ("x3", x3_buf, [TOK, D]), ("d0", dcy_buf[0], [TOK, D]),
            ("d1", dcy_buf[1], [TOK, D]), ("abuf", a_buf, [4, NCH, 128, TOK]),
        ]:
            if nm in dbg_names:
                t = dbg(nm, shape)
                b.dma("gpsimd", smDbg, t.ap(), buf.ap())
        if "DD" not in phases:
            b.dma("gpsimd", smDbg, y_out.ap(), u_buf.ap())
        b.wait("gpsimd", smDbg, b.sems["smDbg"].val)
    elif "DD" not in phases:
        b.barrier()
        smDbg = b.sem("smDbg")
        b.dma("gpsimd", smDbg, y_out.ap(), u_buf.ap())
        b.wait("gpsimd", smDbg, b.sems["smDbg"].val)

    return b.finalize(), b



def make_core_inputs(inputs, core):
    grp, r = core // 4, core % 4
    sA, sB = r, 7 - r
    f32 = np.float32
    x = np.asarray(inputs["L0_attn_x"], f32)
    rows = np.concatenate([np.arange(sA * SEG, (sA + 1) * SEG),
                           np.arange(sB * SEG, (sB + 1) * SEG)])
    d = {}
    d["x_loc"] = np.ascontiguousarray(x[grp][rows])
    ii = lambda n: np.ascontiguousarray(np.asarray(inputs[n], f32))
    d["Wq"] = ii("L0_attn_Wq"); d["Wk"] = ii("L0_attn_Wk"); d["Wv"] = ii("L0_attn_Wv")
    d["Wg"] = ii("L0_attn_gate_W"); d["Wp"] = ii("L0_attn_p_W")
    wck = np.zeros((D, 8), f32); wck[:, 0:1] = ii("L0_attn_clock_W")
    d["Wck"] = wck
    d["Wo"] = ii("L0_attn_out_W")
    d["mW1"] = ii("L0_attn_mlp_W1"); d["mW2"] = ii("L0_attn_mlp_W2")
    d["fW1"] = ii("L0_ffn_W1"); d["fW2"] = ii("L0_ffn_W2")
    d["aWq"] = ii("L1_attn_Wq"); d["aWk"] = ii("L1_attn_Wk"); d["aWv"] = ii("L1_attn_Wv")
    d["aWg"] = ii("L1_attn_Wg"); d["aWgk1"] = ii("L1_attn_Wgk1")
    d["aWgk2"] = ii("L1_attn_Wgk2"); d["aWout"] = ii("L1_attn_Wout")
    d["cnW1"] = ii("L1_ffn_cn_w1"); d["cnW2"] = ii("L1_ffn_cn_w2")
    d["cnW3"] = ii("L1_ffn_cn_w3")
    w4 = np.zeros((256, 8), f32); w4[:, 0:1] = ii("L1_ffn_cn_w4")
    d["cnW4"] = w4
    d["gcW1"] = ii("L1_ffn_gc_w1"); d["gcW2"] = ii("L1_ffn_gc_w2")
    w3 = np.zeros((256, 8), f32); w3[:, 0:1] = ii("L1_ffn_gc_w3")
    d["gcW3"] = w3
    row = lambda n: np.ascontiguousarray(np.asarray(inputs[n], f32).reshape(1, -1))
    d["gate_b"] = row("L0_attn_gate_b"); d["p_b"] = row("L0_attn_p_b")
    ckb = np.zeros((1, 8), f32); ckb[0, 0] = np.asarray(inputs["L0_attn_clock_b"], f32).ravel()[0]
    d["clock_b"] = ckb
    d["out_b"] = row("L0_attn_out_b")
    d["mb1"] = row("L0_attn_mlp_b1"); d["mb2"] = row("L0_attn_mlp_b2")
    d["fb1"] = row("L0_ffn_b1"); d["fb2"] = row("L0_ffn_b2")
    d["bgk2"] = row("L1_attn_bgk2")
    d["cnb1"] = row("L1_ffn_cn_b1"); d["cnb2"] = row("L1_ffn_cn_b2")
    d["cnb3"] = row("L1_ffn_cn_b3")
    c4 = np.zeros((1, 8), f32); c4[0, 0] = np.asarray(inputs["L1_ffn_cn_b4"], f32).ravel()[0]
    d["cnb4"] = c4
    d["gcb1"] = row("L1_ffn_gc_b1"); d["gcb2"] = row("L1_ffn_gc_b2")
    g3 = np.zeros((1, 8), f32); g3[0, 0] = np.asarray(inputs["L1_ffn_gc_b3"], f32).ravel()[0]
    d["gcb3"] = g3
    fm = lambda n: np.ascontiguousarray(np.asarray(inputs[n], f32).reshape(NCH, 128).T)
    d["g1fm"] = fm("L0_attn_ln1_g"); d["b1fm"] = fm("L0_attn_ln1_b")
    d["g2fm"] = fm("L0_attn_ln2_g"); d["b2fm"] = fm("L0_attn_ln2_b")
    d["rmsw"] = np.ascontiguousarray(
        np.broadcast_to(np.asarray(inputs["L1_attn_rms_w"], f32), (128, 256)))
    d["ident"] = np.eye(128, dtype=f32)
    sidx = np.arange(128)[:, None]; tidx = np.arange(128)[None, :]
    d["tri01"] = (sidx <= tidx).astype(f32)
    dm = np.zeros((4, 128, 512), f32)
    for v in range(4):
        pj = np.arange(128)[:, None]; jj = np.arange(512)[None, :]
        dm[v] = np.where(jj <= v * 128 + pj, 0.0, NEG)
    d["dmask4"] = dm
    kc = np.zeros((128, 16), f32)
    for qi, qseg in enumerate((sA, sB)):
        for v in range(8):
            kc[:, qi * 8 + v] = 0.0 if VIS_SEG[v] < qseg else NEG
    d["killc"] = kc
    mkA = np.zeros((128, 64), f32); mkB = np.zeros((128, 64), f32)
    for c in range(8):
        for j in range(8):
            mkA[:, c * 8 + j] = 1.0 if j < sA else 0.0
            mkB[:, c * 8 + j] = 1.0 if j < sB else 0.0
    d["maskA"] = mkA; d["maskB"] = mkB
    m4A = np.zeros((128, 32), f32); m4B = np.zeros((128, 32), f32)
    for c in range(4):
        for j in range(8):
            m4A[:, c * 8 + j] = 1.0 if j < sA else 0.0
            m4B[:, c * 8 + j] = 1.0 if j < sB else 0.0
    d["maskA4"] = m4A; d["maskB4"] = m4B
    pos = rows.astype(np.float64)
    half = 512
    freqs = np.exp(-np.arange(half) * (math.log(10000.0) / half))
    th = pos[:, None] * freqs[None, :]
    d["rope_ch"] = np.cos(th).astype(f32)
    d["rope_sh"] = np.sin(th).astype(f32)
    d["ar_in"] = np.asarray(inputs["L1_ffn_adaptive_range"], f32).reshape(1, 1)
    d["ones"] = np.ones((128, 512), f32)
    return d



def kernel(**inputs):
    nc, b = build_program(dbg_names=(), phases=("A", "B", "M", "C", "DD"))
    in_maps = [make_core_inputs(inputs, c) for c in range(8)]
    res = run_bass_kernel_spmd(nc, in_maps, core_ids=list(range(8)))
    out = np.zeros((BB, T, D), np.float32)
    for c in range(8):
        grp, r = c // 4, c % 4
        sA, sB = r, 7 - r
        y = np.asarray(res.results[c]["y"])
        out[grp, sA * SEG:(sA + 1) * SEG] = y[:SEG]
        out[grp, sB * SEG:(sB + 1) * SEG] = y[SEG:]
    return out
